# version 16
# baseline (speedup 1.0000x reference)
import numpy as np
from contextlib import ExitStack

import concourse.bass as bass
import concourse.mybir as mybir
from concourse.bass_utils import run_bass_kernel_spmd

F32 = mybir.dt.float32
BF16 = mybir.dt.bfloat16
AF = mybir.ActivationFunctionType
OP = mybir.AluOpType
AX = mybir.AxisListType

D = 2048
S = 4096
H = 16
HD = 128
NE = 8
FH = 1024
ND = D // 128
TOWN = 1024
NEG = -1.0e30
EPS = 1e-5
QSCALE = float(HD) ** -0.5

STAGE = 99
DEBUG = False
M_EB = 24
M_T = True

ENGS = ("sync", "tensor", "vector", "scalar", "gpsimd")


class Chan:
    def __init__(self, nc, st, name):
        self.h = st.enter_context(nc.semaphore(name))
        self.n = 0


class Plan:
    def __init__(self, nc, st):
        self.nc = nc
        self.ops = {e: [] for e in ENGS}
        self.ch = {name: Chan(nc, st, "ch_" + name)
                   for name in ("ld", "d0", "d1", "d2", "d3", "d4", "d5", "gd", "sd",
                                "T", "V", "S", "G")}

    def raw(self, eng, fn):
        self.ops[eng].append(fn)

    def v(self, fn):
        self.ops["vector"].append(fn)
        self.ops["vector"].append(lambda e: e.drain())

    def s(self, fn):
        self.ops["scalar"].append(fn)
        self.ops["scalar"].append(lambda e: e.drain())

    def g(self, fn):
        self.ops["gpsimd"].append(fn)
        self.ops["gpsimd"].append(lambda e: e.drain())

    def vdrain(self):
        self.ops["vector"].append(lambda e: e.drain())

    def sdrain(self):
        self.ops["scalar"].append(lambda e: e.drain())

    def dma(self, q, out, in_, ch=None):
        ch = ch or {"sync": "ld", "gpsimd": "gd", "scalar": "sd"}[q]
        c = self.ch[ch]
        c.n += 16
        v = c.n
        self.raw(q, lambda e, out=out, in_=in_, h=c.h:
                 e.dma_start(out=out, in_=in_).then_inc(h, 16))
        return (ch, v)

    def mm(self, psum, lhsT, rhs, start, stop, inc=False):
        if inc:
            c = self.ch["T"]
            c.n += 1
            v = c.n
            self.raw("tensor", lambda e, o=psum, l=lhsT, r=rhs, st_=start, sp=stop, h=c.h:
                     e.matmul(o, l, r, start=st_, stop=sp).then_inc(h, 1))
            return ("T", v)
        self.raw("tensor", lambda e, o=psum, l=lhsT, r=rhs, st_=start, sp=stop:
                 e.matmul(o, l, r, start=st_, stop=sp))
        return None

    def mm3(self, psum, lh, ll, rh, rl, start, stop, inc=False):
        self.mm(psum, lh, rh, start, False)
        self.mm(psum, lh, rl, False, False)
        return self.mm(psum, ll, rh, False, stop, inc=inc)

    def transpose(self, psum, in_, ident, inc=False):
        if inc:
            c = self.ch["T"]
            c.n += 1
            v = c.n
            self.raw("tensor", lambda e, o=psum, i=in_, d=ident, h=c.h:
                     e.transpose(o, i, d).then_inc(h, 1))
            return ("T", v)
        self.raw("tensor", lambda e, o=psum, i=in_, d=ident: e.transpose(o, i, d))
        return None

    def pub(self, eng):
        key = {"vector": "V", "scalar": "S", "gpsimd": "G", "tensor": "T"}[eng]
        c = self.ch[key]
        c.n += 1
        v = c.n
        self.raw(eng, lambda e, h=c.h: e.drain().then_inc(h, 1))
        return (key, v)

    def tpub(self):
        return self.pub("tensor")

    DMA_CH = ("ld", "d0", "d1", "d2", "d3", "d4", "d5", "gd", "sd")

    def wait(self, eng, tok):
        if tok is None:
            return
        ch, v = tok
        if ch in self.DMA_CH:
            v = self.ch[ch].n
        self.raw(eng, lambda e, h=self.ch[ch].h, v=v: e.wait_ge(h, v))

    def barrier(self):
        toks = [self.pub(e) for e in ("vector", "scalar", "gpsimd", "tensor")]
        snap = [(k, self.ch[k].n) for k in self.DMA_CH if self.ch[k].n]
        for eng in ENGS:
            for t in toks:
                self.wait(eng, t)
            for kv in snap:
                self.wait(eng, kv)

    def emit(self, block):
        for eng in ENGS:
            dec = getattr(block, eng)

            def mk(eng_name):
                def body(e):
                    for fn in self.ops[eng_name]:
                        fn(e)
                return body
            dec(mk(eng))


def build_program(debug=False, stage=None):
    stage = STAGE if stage is None else stage
    nc = bass.Bass("TRN2", target_bir_lowering=False)
    st = ExitStack()

    def dram_in(name, shape, dt=F32):
        return nc.dram_tensor(name, shape, dt, kind="ExternalInput").ap()

    def dram_out(name, shape, dt=F32):
        return nc.dram_tensor(name, shape, dt, kind="ExternalOutput").ap()

    def dram_tmp(name, shape, dt):
        return nc.dram_tensor(name, shape, dt, kind="Internal").ap()

    def sb(name, shape, dt, stack=None):
        return (stack or st).enter_context(nc.sbuf_tensor(name, shape, dt))

    def psum(name, shape=(128, 512)):
        return st.enter_context(nc.psum_tensor(name, list(shape), F32))

    xT = dram_in("xT", [D, S])
    wmod_d = dram_in("wmod_d", [D, 6 * D])
    c_fm_d = dram_in("c_fm_d", [128, 16])
    wqkv_t = dram_in("wqkv_t", [48, 128, ND * 128])
    wout_t = dram_in("wout_t", [ND, 128, ND * 128])
    w13_t = dram_in("w13_t", [2 * NE + 2, FH // 128, 128, ND * 128])
    w2_t = dram_in("w2_t", [NE + 1, ND, 128, 8 * 128])
    gwT = dram_in("gwT", [D, NE])
    maskq = dram_in("maskq", [128, 512])
    eye = dram_in("eye", [128, 128])
    out_d = dram_out("out", [D, TOWN])

    dbg = {}
    if debug:
        for nm, shp in [("mod", [128, 96]), ("h1own", [D, TOWN]), ("kc", [128, 512]),
                        ("vt", [128, 128]), ("q", [128, TOWN]), ("y", [D, TOWN]),
                        ("x1", [D, TOWN]), ("h2", [D, TOWN]), ("m", [NE, TOWN])]:
            dbg[nm] = dram_out("dbg_" + nm, shp)

    s_w_hi = dram_tmp("s_w_hi", [48, 128, ND * 128], BF16)
    s_w_lo = dram_tmp("s_w_lo", [48, 128, ND * 128], BF16)
    s_k_hi = dram_tmp("s_k_hi", [H, 128, 8 * 512], BF16)
    s_k_lo = dram_tmp("s_k_lo", [H, 128, 8 * 512], BF16)
    s_v_hi = dram_tmp("s_v_hi", [H, 128, 32 * 128], BF16)
    s_v_lo = dram_tmp("s_v_lo", [H, 128, 32 * 128], BF16)
    s_h1_hi = dram_tmp("s_h1_hi", [ND, 128, TOWN], BF16)
    s_h1_lo = dram_tmp("s_h1_lo", [ND, 128, TOWN], BF16)
    s_q_hi = dram_tmp("s_q_hi", [H, 128, TOWN], BF16)
    s_q_lo = dram_tmp("s_q_lo", [H, 128, TOWN], BF16)

    p = Plan(nc, st)

    PA, PB = psum("PA"), psum("PB")
    PC, PD = psum("PC"), psum("PD")
    PE_, PF = psum("PE"), psum("PF")
    PG, PH = psum("PG"), psum("PH")

    ones_m1 = sb("ones_m1", [128, 1], BF16)
    ones_k1 = sb("ones_k1", [32, 128], F32)
    bcpad = sb("bcpad", [32, 512], F32)
    ident = sb("ident", [128, 128], F32)
    ident_b = sb("ident_b", [128, 128], BF16)
    mask_sb = sb("mask_sb", [128, 512], F32)
    warm = sb("warm", [128, 128], BF16)
    mod_sb = sb("mod_sb", [128, 96], F32)
    ca1 = sb("ca1", [128, 16], F32)
    cb1 = sb("cb1", [128, 16], F32)
    ga_cl = sb("ga_cl", [128, 16], F32)
    gb_cl = sb("gb_cl", [128, 16], F32)
    gw_hi = sb("gw_hi", [128, 16 * NE], BF16)
    gw_lo = sb("gw_lo", [128, 16 * NE], BF16)
    c_fm = sb("c_fm", [128, 16], F32)
    s_fm = sb("s_fm", [128, 16], F32)
    col_a = sb("col_a", [128, 1], F32)
    col_b = sb("col_b", [128, 1], F32)
    col_c = sb("col_c", [128, 1], F32)
    col_d = sb("col_d", [128, 1], F32)
    col_e = sb("col_e", [128, 1], F32)
    eps_col = sb("eps_col", [1, 1], F32)

    t_eye = p.dma("sync", ident[:], eye[:, :])
    p.dma("sync", mask_sb[:], maskq[:, :])
    t_crow = p.dma("sync", c_fm[:], c_fm_d[:, :])
    p.wait("vector", t_eye)
    p.v(lambda e: e.tensor_copy(ident_b[:], ident[:]))

    p.v(lambda e: e.memset(ones_m1[:], 1.0))
    p.v(lambda e: e.memset(ones_k1[:], 0.0))
    p.vdrain()
    p.v(lambda e: e.memset(ones_k1[0:1, :], 1.0))
    p.v(lambda e: e.memset(bcpad[:], 0.0))
    p.v(lambda e: e.memset(warm[:], 0.0))
    p.v(lambda e: e.memset(eps_col[:], EPS))
    tok_init_v = p.pub("vector")

    gw_f = sb("gw_f", [128, 16 * NE], F32)
    gwT_r = gwT.rearrange("(c p) e -> c p e", p=128)
    tks = [p.dma("sync", gw_f[:, k * NE:(k + 1) * NE], gwT_r[k], ch="d0") for k in range(ND)]
    p.wait("vector", tks[-1])
    p.v(lambda e: e.tensor_copy(gw_hi[:], gw_f[:]))
    p.vdrain()
    p.v(lambda e: e.tensor_tensor(gw_lo[:], gw_f[:], gw_hi[:], OP.subtract))
    tok_gw = p.pub("vector")

    p.wait("tensor", tok_init_v)
    p.mm(PA[0:128, 0:128], warm[:], warm[:], True, True)

    def bcast_row(psum_view, n):
        return p.mm(psum_view, ones_k1[:], bcpad[0:32, 0:n], True, True, inc=True)

    stM = ExitStack()
    mod_row = sb("mod_row", [1, 6 * D], F32, stM)
    wm_ld = [sb(f"wmld{i}", [128, 16 * 512], F32, stM) for i in range(2)]
    mscr = [sb(f"mscr{i}", [128, 512], F32, stM) for i in range(2)]

    p.wait("scalar", t_crow)
    p.s(lambda e: e.activation(s_fm[:], c_fm[:], AF.Silu))
    tok_silu = p.pub("scalar")
    p.wait("tensor", tok_silu)

    def load_wm(eb, slot):
        tk = None
        for dc in range(ND):
            tk = p.dma("sync", wm_ld[slot][:, dc * 512:(dc + 1) * 512],
                       wmod_d[dc * 128:(dc + 1) * 128, eb * 512:(eb + 1) * 512],
                       ch=("d2", "d3")[slot])
        return tk

    wm_tok = [load_wm(0, 0), None]
    wm_free = [None, None]
    mbank_free = [None, None]
    for eb in range(M_EB):
        slot = eb % 2
        if eb < M_EB - 1:
            ns = (eb + 1) % 2
            p.wait("sync", wm_free[ns])
            wm_tok[ns] = load_wm(eb + 1, ns)
        p.wait("tensor", wm_tok[slot])
        p.wait("tensor", mbank_free[eb % 2])
        pj = PG if (eb % 2 == 0) else PH
        for dc in range(ND):
            p.mm(pj[0:1, 0:512], s_fm[:, dc:dc + 1],
                 wm_ld[slot][:, dc * 512:(dc + 1) * 512],
                 start=(dc == 0), stop=(dc == ND - 1))
        tok_t = p.tpub()
        wm_free[slot] = tok_t
        p.wait("scalar", tok_t)
        p.s(lambda e, pj=pj, eb=eb: e.activation(
            mod_row[0:1, eb * 512:(eb + 1) * 512], pj[0:1, 0:512], AF.Copy))
        tok_s = p.pub("scalar")
        mbank_free[eb % 2] = tok_s
    tok_s_final = p.pub("scalar")
    p.wait("tensor", tok_s_final)
    for half in range(2 if M_T else 0):
        for r in range(48):
            rr = half * 48 + r
            p.transpose(PG[:, r:r + 1],
                        mod_row[0:1, rr * 128:(rr + 1) * 128], ident[0:1, 0:1])
        tok_t = p.tpub()
        p.wait("vector", tok_t)
        p.v(lambda e, half=half: e.tensor_copy(mod_sb[:, half * 48:(half + 1) * 48],
                                               PG[:, 0:48]))
        tok_v = p.pub("vector")
        p.wait("tensor", tok_v)
    p.vdrain()
    p.v(lambda e: e.tensor_scalar(ca1[:], mod_sb[:, 16:32], 1.0, None, OP.add))
    p.v(lambda e: e.tensor_scalar(cb1[:], mod_sb[:, 64:80], 1.0, None, OP.add))
    tok_v = p.pub("vector")
    p.wait("scalar", tok_v)

    def clamp_gate(dst, lo):
        p.v(lambda e: e.tensor_scalar(mscr[0][:, 16:32], mod_sb[:, lo:lo + 16], 0.0, None,
                                      OP.is_ge))
        p.vdrain()
        p.v(lambda e: e.tensor_scalar(mscr[0][:, 32:48], mscr[0][:, 16:32], 2.0, -1.0,
                                      OP.mult, OP.add))
        tok_vv = p.pub("vector")
        p.wait("scalar", tok_vv)
        p.s(lambda e: e.activation(mscr[0][:, 0:16], mod_sb[:, lo:lo + 16], AF.Abs))
        tok_s = p.pub("scalar")
        p.wait("vector", tok_s)
        p.v(lambda e: e.tensor_scalar(mscr[1][:, 0:16], mscr[0][:, 0:16], 0.25, None, OP.max))
        p.vdrain()
        p.v(lambda e: e.tensor_tensor(dst[:], mscr[0][:, 32:48], mscr[1][:, 0:16], OP.mult))

    clamp_gate(ga_cl, 32)
    clamp_gate(gb_cl, 80)
    tok_mod = p.pub("vector")
    if debug:
        p.wait("scalar", tok_mod)
        p.dma("scalar", dbg["mod"][:, :], mod_sb[:])

    stW = ExitStack()
    w_ld = [sb(f"wld{i}", [128, ND * 128], F32, stW) for i in range(3)]
    w_hi2 = [sb(f"wsplh{i}", [128, ND * 128], BF16, stW) for i in range(3)]
    w_lo2 = [sb(f"wspll{i}", [128, ND * 128], BF16, stW) for i in range(3)]
    p.wait("sync", tok_gw)
    wslot_free = [None, None, None]
    wst_free = [None, None, None]
    WLCH = ("d0", "d1", "ld")
    w_tok = [p.dma("sync", w_ld[i][:], wqkv_t[i], ch=WLCH[i]) for i in range(2)] + [None]
    for sl in range(48):
        slot = sl % 3
        if sl + 2 < 48:
            ns = (sl + 2) % 3
            p.wait("sync", wslot_free[ns])
            w_tok[ns] = p.dma("sync", w_ld[ns][:], wqkv_t[sl + 2], ch=WLCH[ns])
        p.wait("gpsimd", w_tok[slot])
        p.wait("gpsimd", wst_free[slot])
        p.g(lambda e, slot=slot: e.tensor_copy(w_hi2[slot][:], w_ld[slot][:]))
        p.g(lambda e, slot=slot: e.tensor_tensor(w_lo2[slot][:], w_ld[slot][:],
                                                 w_hi2[slot][:], OP.subtract))
        tok_sp = p.pub("gpsimd")
        wslot_free[slot] = tok_sp
        p.wait("sync", tok_sp)
        stch = ("d4", "d5", "d4")[slot]
        p.dma("sync", s_w_hi[sl], w_hi2[slot][:], ch=stch)
        wst_free[slot] = p.dma("sync", s_w_lo[sl], w_lo2[slot][:], ch=stch)

    p.barrier()
    stW.close()
    stM.close()

    if stage >= 2:
        stA = ExitStack()
        xbuf = sb("xbuf", [128, ND * 512], F32, stA)
        xs_hi = sb("xs_hi", [128, 512], BF16, stA)
        xs_lo = sb("xs_lo", [128, 512], BF16, stA)
        x2f = sb("x2f", [128, 512], F32, stA)
        x2_hi = sb("x2_hi", [128, 512], BF16, stA)
        x2_lo = sb("x2_lo", [128, 512], BF16, stA)
        h1f = sb("h1f", [128, 512], F32, stA)
        h1b_hi = [sb(f"h1hi{i}", [128, ND * 512], BF16, stA) for i in range(2)]
        h1b_lo = [sb(f"h1lo{i}", [128, ND * 512], BF16, stA) for i in range(2)]
        kvw_hi = [sb(f"kvwh{i}", [128, ND * 128], BF16, stA) for i in range(2)]
        kvw_lo = [sb(f"kvwl{i}", [128, ND * 128], BF16, stA) for i in range(2)]
        evb_hi = [sb(f"evh{i}", [128, 512], BF16, stA) for i in range(2)]
        evb_lo = [sb(f"evl{i}", [128, 512], BF16, stA) for i in range(2)]
        tvb_hi = [sb(f"tvh{i}", [128, 512], BF16, stA) for i in range(2)]
        tvb_lo = [sb(f"tvl{i}", [128, 512], BF16, stA) for i in range(2)]
        mean_row = sb("mean_row", [1, 512], F32, stA)
        rstd_row = sb("rstd_row", [1, 512], F32, stA)
        var_row = sb("var_row", [1, 512], F32, stA)
        std_row = sb("std_row", [1, 512], F32, stA)

        x_tiles = xT.rearrange("(c p) s -> c p s", p=128)

        def load_x(tt):
            tk = None
            for k in range(ND):
                tk = p.dma("sync", xbuf[:, k * 512:(k + 1) * 512],
                           x_tiles[k][:, tt * 512:(tt + 1) * 512], ch="d0")
            return tk

        def load_kvw(sl, slot):
            p.dma("sync", kvw_hi[slot][:], s_w_hi[16 + sl], ch=("d2", "d3")[slot])
            return p.dma("sync", kvw_lo[slot][:], s_w_lo[16 + sl], ch=("d2", "d3")[slot])

        st_A = {
            "tok_x": load_x(0),
            "kvw_tok": [None, None],
            "kvw_free": [None, None],
            "kv_done": [None, None],
            "g_done": [None, None],
            "h1_tok": [None, None],
            "h1_read": None,
            "ev_free": [None, None],
            "tv_free": [None, None],
            "bank_free": [None, None],
            "pcd_free": None,
            "tpose_pend": None,
        }

        def front_steps(tt):
            par = tt % 2
            p.wait("vector", st_A["tok_x"])
            p.wait("scalar", st_A["tok_x"])
            tok_stats = None
            tok_v2 = None
            p.wait("tensor", st_A["h1_read"])
            for k in range(ND):
                xc = xbuf[:, k * 512:(k + 1) * 512]
                p.v(lambda e, xc=xc: e.tensor_copy(xs_hi[:], xc))
                p.v(lambda e, xc=xc: e.tensor_tensor(xs_lo[:], xc, xs_hi[:], OP.subtract))
                p.wait("scalar", tok_v2)
                p.s(lambda e, xc=xc: e.activation(x2f[:], xc, AF.Square))
                tok_s = p.pub("scalar")
                p.wait("vector", tok_s)
                p.v(lambda e: e.tensor_copy(x2_hi[:], x2f[:]))
                p.v(lambda e: e.tensor_tensor(x2_lo[:], x2f[:], x2_hi[:], OP.subtract))
                tok_v2 = p.pub("vector")
                p.wait("tensor", tok_v2)
                p.mm(PG[0:1, 0:512], ones_m1[:], xs_hi[:], k == 0, False)
                p.mm(PG[0:1, 0:512], ones_m1[:], xs_lo[:], False, k == ND - 1)
                p.mm(PH[0:1, 0:512], ones_m1[:], x2_hi[:], k == 0, False)
                tok_stats = p.mm(PH[0:1, 0:512], ones_m1[:], x2_lo[:], False, k == ND - 1,
                                 inc=(k == ND - 1))
                tok_t2 = p.tpub()
                p.wait("vector", tok_t2)
                yield
            p.wait("vector", tok_stats)
            p.v(lambda e: e.tensor_scalar(mean_row[:], PG[0:1, 0:512], 1.0 / D, None, OP.mult))
            p.v(lambda e: e.tensor_scalar(var_row[:], PH[0:1, 0:512], 1.0 / D, None, OP.mult))
            p.vdrain()
            p.v(lambda e: e.tensor_tensor(x2f[0:1, 0:512], mean_row[:], mean_row[:], OP.mult))
            p.vdrain()
            p.v(lambda e: e.tensor_tensor(var_row[:], var_row[:], x2f[0:1, 0:512], OP.subtract))
            tok_v = p.pub("vector")
            p.wait("scalar", tok_v)
            p.s(lambda e: e.activation(std_row[:], var_row[:], AF.Sqrt,
                                       bias=eps_col[0:1, 0:1]))
            tok_s = p.pub("scalar")
            p.wait("vector", tok_s)
            p.v(lambda e: e.reciprocal(rstd_row[:], std_row[:]))
            p.v(lambda e: e.tensor_copy(bcpad[0:1, 0:512], mean_row[:]))
            tok_v = p.pub("vector")
            p.wait("tensor", tok_v)
            bcast_row(PG[:, 0:512], 512)
            tok_t = p.tpub()
            p.wait("vector", tok_t)
            p.v(lambda e: e.tensor_copy(bcpad[0:1, 0:512], rstd_row[:]))
            tok_v = p.pub("vector")
            p.wait("tensor", tok_v)
            tok_t = bcast_row(PH[:, 0:512], 512)
            p.wait("vector", tok_t)
            hh, hl = h1b_hi[par], h1b_lo[par]
            p.wait("vector", tok_mod)
            p.wait("vector", st_A["kv_done"][par])
            p.wait("vector", st_A["g_done"][par])
            for k in range(ND):
                xc = xbuf[:, k * 512:(k + 1) * 512]
                p.v(lambda e, xc=xc: e.tensor_tensor(h1f[:], xc, PG[:, 0:512], OP.subtract))
                p.v(lambda e: e.tensor_tensor(h1f[:], h1f[:], PH[:, 0:512], OP.mult))
                p.v(lambda e, k=k: e.tensor_scalar(h1f[:], h1f[:], ca1[:, k:k + 1],
                                                   mod_sb[:, k:k + 1], OP.mult, OP.add))
                p.v(lambda e, k=k, hh=hh: e.tensor_copy(hh[:, k * 512:(k + 1) * 512], h1f[:]))
                p.v(lambda e, k=k, hh=hh, hl=hl: e.tensor_tensor(
                    hl[:, k * 512:(k + 1) * 512], h1f[:], hh[:, k * 512:(k + 1) * 512],
                    OP.subtract))
                yield
            tok_h1 = p.pub("vector")
            st_A["h1_tok"][par] = tok_h1
            st_A["h1_read"] = tok_h1
            if tt < 7:
                p.wait("sync", tok_h1)
                st_A["tok_x"] = load_x(tt + 1)
            p.wait("gpsimd", tok_h1)
            gd = None
            for k in range(ND):
                p.dma("gpsimd", s_h1_hi[k][:, tt * 128:(tt + 1) * 128],
                      hh[:, k * 512:k * 512 + 128])
                gd = p.dma("gpsimd", s_h1_lo[k][:, tt * 128:(tt + 1) * 128],
                           hl[:, k * 512:k * 512 + 128])
            st_A["g_done"][par] = gd

        def flush_tpose():
            pend = st_A["tpose_pend"]
            if pend is None:
                return
            eslot, h_, tt_ = pend
            st_A["tpose_pend"] = None
            p.wait("tensor", st_A["tv_free"][eslot])
            p.wait("tensor", st_A["pcd_free"])
            for q4 in range(4):
                p.transpose(PC[:, q4 * 64:(q4 + 1) * 64].bitcast(BF16),
                            evb_hi[eslot][:, q4 * 128:(q4 + 1) * 128], ident_b[:])
            for q4 in range(4):
                p.transpose(PD[:, q4 * 64:(q4 + 1) * 64].bitcast(BF16),
                            evb_lo[eslot][:, q4 * 128:(q4 + 1) * 128], ident_b[:])
            tok_t2 = p.tpub()
            st_A["ev_free"][eslot] = tok_t2
            p.wait("vector", tok_t2)
            tv_h, tv_l = tvb_hi[eslot], tvb_lo[eslot]
            p.v(lambda e, tv_h=tv_h: e.tensor_copy(tv_h[:], PC[:, 0:256].bitcast(BF16)))
            p.v(lambda e, tv_l=tv_l: e.tensor_copy(tv_l[:], PD[:, 0:256].bitcast(BF16)))
            tok_v2 = p.pub("vector")
            st_A["pcd_free"] = tok_v2
            p.wait("gpsimd", tok_v2)
            p.dma("gpsimd", s_v_hi[h_][:, tt_ * 512:(tt_ + 1) * 512], tv_h[:])
            tok_g = p.dma("gpsimd", s_v_lo[h_][:, tt_ * 512:(tt_ + 1) * 512], tv_l[:])
            st_A["tv_free"][eslot] = tok_g

        def kv_steps(tt):
            par = tt % 2
            hh, hl = h1b_hi[par], h1b_lo[par]
            p.wait("tensor", st_A["h1_tok"][par])
            if st_A["kvw_tok"][0] is None:
                st_A["kvw_tok"][0] = load_kvw(0, 0)
            kv_last = None
            for sl in range(32):
                wslot = sl % 2
                if sl < 31:
                    ns = (sl + 1) % 2
                    p.wait("sync", st_A["kvw_free"][ns])
                    st_A["kvw_tok"][ns] = load_kvw(sl + 1, ns)
                elif tt < 7:
                    ns = (sl + 1) % 2
                    p.wait("sync", st_A["kvw_free"][ns])
                    st_A["kvw_tok"][ns] = load_kvw(0, ns)
                p.wait("tensor", st_A["kvw_tok"][wslot])
                p.wait("tensor", st_A["bank_free"][sl % 2])
                pj = PA if (sl % 2 == 0) else PB
                for k in range(ND):
                    p.mm3(pj[:, 0:512],
                          kvw_hi[wslot][:, k * 128:(k + 1) * 128],
                          kvw_lo[wslot][:, k * 128:(k + 1) * 128],
                          hh[:, k * 512:(k + 1) * 512], hl[:, k * 512:(k + 1) * 512],
                          start=(k == 0), stop=(k == ND - 1))
                tok_t = p.tpub()
                st_A["kvw_free"][wslot] = tok_t
                kv_last = tok_t
                flush_tpose()
                p.wait("vector", tok_t)
                eslot = sl % 2
                if sl < 16:
                    h_ = sl
                    p.wait("vector", st_A["ev_free"][eslot])
                    ev_h, ev_l = evb_hi[eslot], evb_lo[eslot]
                    p.v(lambda e, pj=pj, ev_h=ev_h: e.tensor_copy(ev_h[:], pj[:, 0:512]))
                    p.v(lambda e, pj=pj, ev_h=ev_h, ev_l=ev_l: e.tensor_tensor(
                        ev_l[:], pj[:, 0:512], ev_h[:], OP.subtract))
                    tok_v = p.pub("vector")
                    st_A["bank_free"][sl % 2] = tok_v
                    p.wait("gpsimd", tok_v)
                    p.dma("gpsimd", s_k_hi[h_][:, tt * 512:(tt + 1) * 512], ev_h[:])
                    tok_g = p.dma("gpsimd", s_k_lo[h_][:, tt * 512:(tt + 1) * 512], ev_l[:])
                    st_A["ev_free"][eslot] = tok_g
                else:
                    h_ = sl - 16
                    p.wait("vector", st_A["ev_free"][eslot])
                    ev_h, ev_l = evb_hi[eslot], evb_lo[eslot]
                    p.v(lambda e, pj=pj, ev_h=ev_h: e.tensor_copy(ev_h[:], pj[:, 0:512]))
                    p.v(lambda e, pj=pj, ev_h=ev_h, ev_l=ev_l: e.tensor_tensor(
                        ev_l[:], pj[:, 0:512], ev_h[:], OP.subtract))
                    tok_v = p.pub("vector")
                    st_A["bank_free"][sl % 2] = tok_v
                    st_A["ev_free"][eslot] = tok_v
                    p.wait("tensor", tok_v)
                    st_A["tpose_pend"] = (eslot, h_, tt)
                yield
            flush_tpose()
            st_A["kv_done"][par] = kv_last

        def drive(*gens):
            gens = [g for g in gens if g is not None]
            while gens:
                nxt = []
                for g_ in gens:
                    try:
                        next(g_)
                        nxt.append(g_)
                    except StopIteration:
                        pass
                gens = nxt

        for tt in range(8):
            drive(front_steps(tt), kv_steps(tt - 1) if tt >= 1 else None)
        drive(kv_steps(7))
        p.barrier()
        stA.close()

        if debug:
            stD = ExitStack()
            hb = sb("hb_dbg", [128, TOWN], BF16, stD)
            hb2 = sb("hb2_dbg", [128, TOWN], BF16, stD)
            hf = sb("hf_dbg", [128, TOWN], F32, stD)
            hb_free = None
            hb_free2 = None
            for k in range(ND):
                p.wait("sync", hb_free)
                t1 = p.dma("sync", hb[:], s_h1_hi[k])
                t2 = p.dma("sync", hb2[:], s_h1_lo[k])
                p.wait("vector", t2)
                p.wait("vector", hb_free2)
                p.v(lambda e: e.tensor_tensor(hf[:], hb[:], hb2[:], OP.add))
                tok_v = p.pub("vector")
                hb_free = tok_v
                p.wait("scalar", tok_v)
                tok_d = p.dma("scalar", dbg["h1own"][k * 128:(k + 1) * 128, :], hf[:])
                p.wait("scalar", tok_d)
                hb_free2 = tok_d
            p.barrier()
            stD.close()

    if stage >= 3:
        stQ = ExitStack()
        h1o_hi = sb("h1o_hi", [128, ND * TOWN], BF16, stQ)
        h1o_lo = sb("h1o_lo", [128, ND * TOWN], BF16, stQ)
        qw_hi = [sb(f"qwh{i}", [128, ND * 128], BF16, stQ) for i in range(2)]
        qw_lo = [sb(f"qwl{i}", [128, ND * 128], BF16, stQ) for i in range(2)]
        qf = sb("qf", [128, 512], F32, stQ)
        q_hi = sb("q_hi", [128, 512], BF16, stQ)
        q_lo = sb("q_lo", [128, 512], BF16, stQ)
        tk = None
        for k in range(ND):
            p.dma("sync", h1o_hi[:, k * TOWN:(k + 1) * TOWN], s_h1_hi[k])
            tk = p.dma("sync", h1o_lo[:, k * TOWN:(k + 1) * TOWN], s_h1_lo[k])
        p.wait("tensor", tk)

        def load_qw(h_, slot):
            p.dma("sync", qw_hi[slot][:], s_w_hi[h_], ch=("d0", "d1")[slot])
            return p.dma("sync", qw_lo[slot][:], s_w_lo[h_], ch=("d0", "d1")[slot])

        qw_tok = [load_qw(0, 0), None]
        qw_free = [None, None]
        qsp_free = None
        bank_free = [None, None]
        for h_ in range(H):
            slot = h_ % 2
            if h_ < H - 1:
                ns = (h_ + 1) % 2
                p.wait("sync", qw_free[ns])
                qw_tok[ns] = load_qw(h_ + 1, ns)
            p.wait("tensor", qw_tok[slot])
            for qc in range(2):
                p.wait("tensor", bank_free[qc])
                pj = PA if qc == 0 else PB
                for k in range(ND):
                    p.mm3(pj[:, 0:512],
                          qw_hi[slot][:, k * 128:(k + 1) * 128],
                          qw_lo[slot][:, k * 128:(k + 1) * 128],
                          h1o_hi[:, k * TOWN + qc * 512: k * TOWN + (qc + 1) * 512],
                          h1o_lo[:, k * TOWN + qc * 512: k * TOWN + (qc + 1) * 512],
                          start=(k == 0), stop=(k == ND - 1))
                tok_t = p.tpub()
                qw_free[slot] = tok_t
                p.wait("scalar", tok_t)
                p.wait("scalar", qsp_free)
                p.s(lambda e, pj=pj: e.activation(qf[:], pj[:, 0:512], AF.Copy,
                                                              scale=QSCALE, bias=0.0))
                tok_s = p.pub("scalar")
                bank_free[qc] = tok_s
                p.wait("vector", tok_s)
                p.v(lambda e: e.tensor_copy(q_hi[:], qf[:]))
                p.v(lambda e: e.tensor_tensor(q_lo[:], qf[:], q_hi[:], OP.subtract))
                tok_v = p.pub("vector")
                p.wait("gpsimd", tok_v)
                p.dma("gpsimd", s_q_hi[h_][:, qc * 512:(qc + 1) * 512], q_hi[:])
                tok_g = p.dma("gpsimd", s_q_lo[h_][:, qc * 512:(qc + 1) * 512], q_lo[:])
                qsp_free = tok_g
                p.wait("vector", tok_g)
                if debug and h_ == 1:
                    p.wait("scalar", tok_g)
                    tok_d = p.dma("scalar", dbg["q"][:, qc * 512:(qc + 1) * 512], qf[:])
                    p.wait("scalar", tok_d)
        p.barrier()
        stQ.close()

    stBO = ExitStack()
    y_hi = sb("y_hi", [128, H * TOWN], BF16, stBO)
    y_lo = sb("y_lo", [128, H * TOWN], BF16, stBO)

    if stage >= 4:
        stB = ExitStack()
        qb_hi = [sb(f"qbh{i}", [128, TOWN], BF16, stB) for i in range(2)]
        qb_lo = [sb(f"qbl{i}", [128, TOWN], BF16, stB) for i in range(2)]
        kb_hi = [sb(f"kbh{i}", [128, 8 * 512], BF16, stB) for i in range(2)]
        kb_lo = [sb(f"kbl{i}", [128, 8 * 512], BF16, stB) for i in range(2)]
        vb_hi = sb("vbh", [128, 32 * 128], BF16, stB)
        vb_lo = sb("vbl", [128, 32 * 128], BF16, stB)
        scpn = [sb(f"scpn{i}", [128, 8 * 512], F32, stB) for i in range(2)]
        ptb_hi = [sb(f"pth{i}", [128, 8 * 512], BF16, stB) for i in range(2)]
        ptb_lo = [sb(f"ptl{i}", [128, 8 * 512], BF16, stB) for i in range(2)]
        yf_ = sb("yf_", [128, 128], F32, stB)
        yg_ = sb("yg_", [128, 128], F32, stB)
        mxc = [sb(f"mxc{i}", [128, 1], F32, stB) for i in range(2)]
        ngc = [sb(f"ngc{i}", [128, 1], F32, stB) for i in range(2)]
        dnc = [sb(f"dnc{i}", [128, 1], F32, stB) for i in range(2)]
        rcc = [sb(f"rcc{i}", [128, 1], F32, stB) for i in range(2)]

        def load_qk(h_, slot):
            c = ("d0", "d1")[slot]
            p.dma("sync", qb_hi[slot][:], s_q_hi[h_], ch=c)
            p.dma("sync", qb_lo[slot][:], s_q_lo[h_], ch=c)
            p.dma("sync", kb_hi[slot][:], s_k_hi[h_], ch=c)
            return p.dma("sync", kb_lo[slot][:], s_k_lo[h_], ch=c)

        def load_v(h_):
            p.dma("sync", vb_hi[:], s_v_hi[h_], ch="d2")
            return p.dma("sync", vb_lo[:], s_v_lo[h_], ch="d2")

        stb = {
            "qk_tok": [load_qk(0, 0), None],
            "qk_free": [None, None],
            "v_tok": load_v(0),
            "v_free": None,
            "sbank": [None, None],
            "tbank": [None, None],
            "ybank": [None, None],
            "scpn_free": [None, None],
            "pt_free": [None, None],
            "exp_tok": [None, None],
            "rcp_tok": [None, None],
            "dbank": None,
            "bcp": None,
        }

        def sm_steps(h_, m, slot):
            par = m % 2
            W = m + 1
            qh_, ql_ = qb_hi[slot], qb_lo[slot]
            kh_, kl_ = kb_hi[slot], kb_lo[slot]
            lh = qh_[:, m * 128:(m + 1) * 128]
            ll = ql_[:, m * 128:(m + 1) * 128]
            row = scpn[par]
            first = True
            last_tok = None
            for w in range(W):
                p.wait("tensor", stb["sbank"][w % 2])
                pj = PA if (w % 2 == 0) else PB
                tok_t = p.mm3(pj[:, 0:512], lh, ll,
                              kh_[:, w * 512:(w + 1) * 512], kl_[:, w * 512:(w + 1) * 512],
                              start=True, stop=True, inc=True)
                last_tok = tok_t
                if w == W - 1:
                    p.wait("vector", tok_t)
                    p.v(lambda e, pj=pj: e.tensor_tensor(pj[:, 0:512], pj[:, 0:512],
                                                         mask_sb[:], OP.add))
                    tok_t = p.pub("vector")
                p.wait("scalar", tok_t)
                if first:
                    p.wait("scalar", stb["scpn_free"][par])
                    first = False
                p.s(lambda e, pj=pj, w=w, row=row:
                    e.activation(row[:, w * 512:(w + 1) * 512], pj[:, 0:512], AF.Copy))
                tok_s = p.pub("scalar")
                stb["sbank"][w % 2] = tok_s
                yield
            if m == 7:
                stb["qk_free"][slot] = last_tok
            p.wait("vector", tok_s)
            p.v(lambda e, W=W, row=row: e.tensor_reduce(mxc[par][:], row[:, 0:W * 512],
                                                        AX.X, OP.max))
            p.vdrain()
            p.v(lambda e: e.tensor_scalar(ngc[par][:], mxc[par][:], -1.0, None, OP.mult))
            tok_v = p.pub("vector")
            p.wait("scalar", tok_v)
            p.s(lambda e, W=W, row=row: e.activation(row[:, 0:W * 512], row[:, 0:W * 512],
                                                     AF.Exp, bias=ngc[par][:, 0:1],
                                                     accum_out=dnc[par][:, 0:1]))
            tok_s = p.pub("scalar")
            stb["exp_tok"][par] = tok_s
            p.wait("vector", tok_s)
            p.v(lambda e: e.reciprocal(rcc[par][:], dnc[par][:]))
            stb["rcp_tok"][par] = p.pub("vector")
            yield

        def back_steps(h_, m):
            par = m % 2
            W = m + 1
            row = scpn[par]
            pth, ptl = ptb_hi[par], ptb_lo[par]
            p.wait("tensor", stb["exp_tok"][par])
            tok_v = None
            tok_t = None
            for w in range(W):
                p.wait("tensor", stb["tbank"][w % 2])
                pj = PC if (w % 2 == 0) else PD
                for q4 in range(4):
                    p.transpose(pj[:, q4 * 128:(q4 + 1) * 128],
                                row[:, w * 512 + q4 * 128: w * 512 + (q4 + 1) * 128],
                                ident[:])
                tok_t = p.tpub()
                p.wait("vector", tok_t)
                if w == 0:
                    p.wait("vector", stb["pt_free"][par])
                p.v(lambda e, pj=pj, w=w, pth=pth: e.tensor_copy(
                    pth[:, w * 512:(w + 1) * 512], pj[:, 0:512]))
                p.v(lambda e, pj=pj, w=w, pth=pth, ptl=ptl: e.tensor_tensor(
                    ptl[:, w * 512:(w + 1) * 512], pj[:, 0:512],
                    pth[:, w * 512:(w + 1) * 512], OP.subtract))
                tok_v = p.pub("vector")
                stb["tbank"][w % 2] = tok_v
                yield
            stb["scpn_free"][par] = tok_t
            p.wait("tensor", tok_v)
            yb = (PE_, PF)[par]
            p.wait("tensor", stb["ybank"][par])
            for kt in range(4 * W):
                p.mm3(yb[:, 0:128],
                      vb_hi[:, kt * 128:(kt + 1) * 128], vb_lo[:, kt * 128:(kt + 1) * 128],
                      pth[:, kt * 128:(kt + 1) * 128], ptl[:, kt * 128:(kt + 1) * 128],
                      start=(kt == 0), stop=(kt == 4 * W - 1))
            tok_av = p.tpub()
            stb["pt_free"][par] = tok_av
            if m == 7:
                stb["v_free"] = tok_av
            p.wait("tensor", stb["rcp_tok"][par])
            p.wait("tensor", stb["dbank"])
            p.transpose(PH[0:1, 0:128], rcc[par][:], ident[:])
            tok_t = p.tpub()
            p.wait("vector", tok_t)
            p.wait("vector", stb["bcp"])
            p.v(lambda e: e.tensor_copy(bcpad[0:1, 0:128], PH[0:1, 0:128]))
            tok_v = p.pub("vector")
            p.wait("tensor", tok_v)
            tok_bc = bcast_row(PG[:, 0:128], 128)
            stb["bcp"] = tok_bc
            p.wait("vector", tok_bc)
            p.wait("vector", tok_av)
            dst = h_ * TOWN + m * 128
            p.v(lambda e, yb=yb: e.tensor_copy(yg_[:], yb[:, 0:128]))
            p.vdrain()
            p.v(lambda e: e.tensor_tensor(yf_[:], yg_[:], PG[:, 0:128], OP.mult))
            p.vdrain()
            p.v(lambda e, dst=dst: e.tensor_copy(y_hi[:, dst:dst + 128], yf_[:]))
            p.vdrain()
            p.v(lambda e, dst=dst: e.tensor_tensor(y_lo[:, dst:dst + 128], yf_[:],
                                                   y_hi[:, dst:dst + 128], OP.subtract))
            tok_v = p.pub("vector")
            stb["ybank"][par] = tok_v
            stb["dbank"] = tok_v
            yield

        def drive_b(*gens):
            gens = [g_ for g_ in gens if g_ is not None]
            while gens:
                nxt = []
                for g_ in gens:
                    try:
                        next(g_)
                        nxt.append(g_)
                    except StopIteration:
                        pass
                gens = nxt

        for h_ in range(H):
            slot = h_ % 2
            if h_ < H - 1:
                ns = (h_ + 1) % 2
                p.wait("sync", stb["qk_free"][ns])
                stb["qk_tok"][ns] = load_qk(h_ + 1, ns)
            p.wait("tensor", stb["qk_tok"][slot])
            p.wait("tensor", stb["v_tok"])
            for m in range(8):
                drive_b(sm_steps(h_, m, slot),
                        back_steps(h_, m - 1) if m >= 1 else None)
            drive_b(back_steps(h_, 7))
            if h_ < H - 1:
                p.wait("sync", stb["v_free"])
                stb["v_tok"] = load_v(h_ + 1)
        p.barrier()
        stB.close()

        if debug:
            stD = ExitStack()
            yf = sb("yf_dbg", [128, TOWN], F32, stD)
            for h_ in range(H):
                p.v(lambda e, h_=h_: e.tensor_tensor(yf[:], y_hi[:, h_ * TOWN:(h_ + 1) * TOWN],
                                                     y_lo[:, h_ * TOWN:(h_ + 1) * TOWN], OP.add))
                tok_v = p.pub("vector")
                p.wait("scalar", tok_v)
                tok_d = p.dma("scalar", dbg["y"][h_ * 128:(h_ + 1) * 128, :], yf[:])
                p.wait("vector", tok_d)
            p.barrier()
            stD.close()

    s_x1 = dram_tmp("s_x1", [ND, 128, TOWN], F32)
    if stage >= 5:
        stO = ExitStack()
        ow_ld = [sb(f"owld{i}", [128, ND * 128], F32, stO) for i in range(2)]
        ow_hi = sb("ow_hi", [128, ND * 128], BF16, stO)
        ow_lo = sb("ow_lo", [128, ND * 128], BF16, stO)
        xof = sb("xof", [128, 512], F32, stO)
        x1t = sb("x1t", [128, TOWN], F32, stO)
        x_tiles2 = xT.rearrange("(c p) s -> c p s", p=128)

        def load_ow(ec, slot):
            return p.dma("sync", ow_ld[slot][:], wout_t[ec], ch=("d0", "d1")[slot])

        ow_tok = [load_ow(0, 0), None]
        ow_free = [None, None]
        owsp_free = None
        obank_free = [None, None]
        xof_free = None
        x1t_free = None
        for ec in range(ND):
            slot = ec % 2
            if ec < ND - 1:
                ns = (ec + 1) % 2
                p.wait("sync", ow_free[ns])
                ow_tok[ns] = load_ow(ec + 1, ns)
            p.wait("gpsimd", ow_tok[slot])
            p.wait("gpsimd", owsp_free)
            p.g(lambda e, slot=slot: e.tensor_copy(ow_hi[:], ow_ld[slot][:]))
            p.g(lambda e, slot=slot: e.tensor_tensor(ow_lo[:], ow_ld[slot][:], ow_hi[:],
                                                     OP.subtract))
            tok_v = p.pub("gpsimd")
            ow_free[slot] = tok_v
            p.wait("tensor", tok_v)
            for qc in range(2):
                p.wait("tensor", obank_free[qc])
                pj = PA if qc == 0 else PB
                for k in range(ND):
                    p.mm3(pj[:, 0:512],
                          ow_hi[:, k * 128:(k + 1) * 128], ow_lo[:, k * 128:(k + 1) * 128],
                          y_hi[:, k * TOWN + qc * 512: k * TOWN + (qc + 1) * 512],
                          y_lo[:, k * TOWN + qc * 512: k * TOWN + (qc + 1) * 512],
                          start=(k == 0), stop=(k == ND - 1))
                tok_t = p.tpub()
                tok_x = []
                for t8 in range(4):
                    p.wait("sync", xof_free)
                    tok_x.append(p.dma("sync", xof[:, t8 * 128:(t8 + 1) * 128],
                                       x_tiles2[ec][:, (qc * 4 + t8) * 512:
                                                    (qc * 4 + t8) * 512 + 128], ch="d2"))
                p.wait("vector", tok_t)
                for t in tok_x:
                    p.wait("vector", t)
                if qc == 0:
                    p.wait("vector", x1t_free)
                p.v(lambda e, pj=pj, ec=ec, qc=qc:
                    e.tensor_scalar(x1t[:, qc * 512:(qc + 1) * 512], pj[:, 0:512],
                                    ga_cl[:, ec:ec + 1], None, OP.mult))
                p.v(lambda e, qc=qc: e.tensor_tensor(x1t[:, qc * 512:(qc + 1) * 512],
                                                     x1t[:, qc * 512:(qc + 1) * 512],
                                                     xof[:], OP.add))
                tok_v = p.pub("vector")
                obank_free[qc] = tok_v
                xof_free = tok_v
            owsp_free = tok_t
            p.wait("gpsimd", tok_v)
            x1t_free = p.dma("gpsimd", s_x1[ec], x1t[:])
        p.barrier()
        stO.close()
    stBO.close()

    x1 = sb("x1", [128, ND * TOWN], F32)
    p.v(lambda e: e.memset(x1[:], 0.0))
    tok_x1init = p.pub("vector")
    if stage >= 5:
        p.wait("sync", tok_x1init)
        tk = None
        for k in range(ND):
            tk = p.dma("sync", x1[:, k * TOWN:(k + 1) * TOWN], s_x1[k])
        p.wait("vector", tk)
        p.wait("tensor", tk)
        p.wait("scalar", tk)
        if debug:
            for k in range(ND):
                tok_d = p.dma("scalar", dbg["x1"][k * 128:(k + 1) * 128, :],
                              x1[:, k * TOWN:(k + 1) * TOWN])
            p.wait("scalar", tok_d)
            p.barrier()

    if stage >= 5:
        stF = ExitStack()
        h2_hi = sb("h2_hi", [128, ND * 512], BF16, stF)
        h2_lo = sb("h2_lo", [128, ND * 512], BF16, stF)
        l_hi = sb("l_hi", [128, 512], BF16, stF)
        l_lo = sb("l_lo", [128, 512], BF16, stF)
        l2f = sb("l2f", [128, 512], F32, stF)
        l2_hi = sb("l2_hi", [128, 512], BF16, stF)
        l2_lo = sb("l2_lo", [128, 512], BF16, stF)
        lh1f = sb("lh1f", [128, 512], F32, stF)
        lmean = sb("lmean", [1, 512], F32, stF)
        lvar = sb("lvar", [1, 512], F32, stF)
        lrstd = sb("lrstd", [1, 512], F32, stF)
        lstd = sb("lstd", [1, 512], F32, stF)
        sc_tm = sb("sc_tm", [128, 4 * NE], F32, stF)
        pick1 = sb("g_pick1", [128, 4 * NE], F32, stF)
        pick2 = sb("g_pick2", [128, 4 * NE], F32, stF)
        scm = sb("g_scm", [128, 4 * NE], F32, stF)
        fnd = sb("g_fnd", [128, 4], F32, stF)
        mrow = sb("g_mrow", [128, 4], F32, stF)
        m_rows = sb("m_rows", [1, NE * 512], F32, stF)
        w13_ld = [sb(f"w13ld{i}", [128, ND * 128], F32, stF) for i in range(2)]
        w13_hi = [sb(f"w13hi{i}", [128, ND * 128], BF16, stF) for i in range(2)]
        w13_lo = [sb(f"w13lo{i}", [128, ND * 128], BF16, stF) for i in range(2)]
        w2_ld = [sb(f"w2ld{i}", [128, 8 * 128], F32, stF) for i in range(2)]
        w2s_hi = [sb(f"w2s_hi{i}", [128, 8 * 128], BF16, stF) for i in range(2)]
        w2s_lo = [sb(f"w2s_lo{i}", [128, 8 * 128], BF16, stF) for i in range(2)]
        g_hi = sb("g_hi", [128, 8 * 512], BF16, stF)
        g_lo = sb("g_lo", [128, 8 * 512], BF16, stF)
        mb = sb("mb", [128, 512], F32, stF)
        af = sb("af", [128, 512], F32, stF)
        bf = l2f
        sf = lh1f

        def load_w13(e_, ft, which, slot):
            idx = (2 * e_ + which) if e_ < NE else (16 + which)
            return p.dma("sync", w13_ld[slot][:], w13_t[idx, ft], ch=("d0", "d1")[slot])

        def load_w2(e_, dc, slot):
            idx = e_ if e_ < NE else NE
            return p.dma("sync", w2_ld[slot][:], w2_t[idx, dc], ch=("d2", "d3")[slot])

        def ecol(buf, e_):
            return buf[:].rearrange("p (t e) -> p t e", e=NE)[:, :, e_:e_ + 1]

        gta = sb("gta", [128, 4], F32, stF)
        gtb = sb("gtb", [128, 4], F32, stF)
        gtc = sb("gtc", [128, 4], F32, stF)
        gfa = sb("gfa", [128, 4], F32, stF)
        gfb = sb("gfb", [128, 4], F32, stF)
        gtmp = sb("gtmp", [128, 4 * NE], F32, stF)

        w13_free = [None, None]
        w2_free = [None, None]
        hbank_free = [None, None]
        rbank_free = [None, None]
        mbank2_free = None
        mb_free = None
        h2_free = None
        for tc in range(2):
            xoff = tc * 512
            tok_stats = None
            tok_v2 = None
            p.wait("vector", h2_free)
            for k in range(ND):
                xc = x1[:, k * TOWN + xoff: k * TOWN + xoff + 512]
                p.v(lambda e, xc=xc: e.tensor_copy(l_hi[:], xc))
                p.v(lambda e, xc=xc: e.tensor_tensor(l_lo[:], xc, l_hi[:], OP.subtract))
                p.wait("scalar", tok_v2)
                p.s(lambda e, xc=xc: e.activation(l2f[:], xc, AF.Square))
                tok_s = p.pub("scalar")
                p.wait("vector", tok_s)
                p.v(lambda e: e.tensor_copy(l2_hi[:], l2f[:]))
                p.v(lambda e: e.tensor_tensor(l2_lo[:], l2f[:], l2_hi[:], OP.subtract))
                tok_v2 = p.pub("vector")
                p.wait("tensor", tok_v2)
                p.mm(PG[0:1, 0:512], ones_m1[:], l_hi[:], k == 0, False)
                p.mm(PG[0:1, 0:512], ones_m1[:], l_lo[:], False, k == ND - 1)
                p.mm(PH[0:1, 0:512], ones_m1[:], l2_hi[:], k == 0, False)
                tok_stats = p.mm(PH[0:1, 0:512], ones_m1[:], l2_lo[:], False, k == ND - 1,
                                 inc=(k == ND - 1))
                tok_t2 = p.tpub()
                p.wait("vector", tok_t2)
            p.wait("vector", tok_stats)
            p.v(lambda e: e.tensor_scalar(lmean[:], PG[0:1, 0:512], 1.0 / D, None, OP.mult))
            p.v(lambda e: e.tensor_scalar(lvar[:], PH[0:1, 0:512], 1.0 / D, None, OP.mult))
            p.vdrain()
            p.v(lambda e: e.tensor_tensor(l2f[0:1, 0:512], lmean[:], lmean[:], OP.mult))
            p.vdrain()
            p.v(lambda e: e.tensor_tensor(lvar[:], lvar[:], l2f[0:1, 0:512], OP.subtract))
            tok_v = p.pub("vector")
            p.wait("scalar", tok_v)
            p.s(lambda e: e.activation(lstd[:], lvar[:], AF.Sqrt,
                                                   bias=eps_col[0:1, 0:1]))
            tok_s = p.pub("scalar")
            p.wait("vector", tok_s)
            p.v(lambda e: e.reciprocal(lrstd[:], lstd[:]))
            p.v(lambda e: e.tensor_copy(bcpad[0:1, 0:512], lmean[:]))
            tok_v = p.pub("vector")
            p.wait("tensor", tok_v)
            bcast_row(PG[:, 0:512], 512)
            tok_t = p.tpub()
            p.wait("vector", tok_t)
            p.v(lambda e: e.tensor_copy(bcpad[0:1, 0:512], lrstd[:]))
            tok_v = p.pub("vector")
            p.wait("tensor", tok_v)
            tok_t = bcast_row(PH[:, 0:512], 512)
            p.wait("vector", tok_t)
            for k in range(ND):
                xc = x1[:, k * TOWN + xoff: k * TOWN + xoff + 512]
                dst = k * 512
                p.v(lambda e, xc=xc: e.tensor_tensor(lh1f[:], xc, PG[:, 0:512], OP.subtract))
                p.v(lambda e: e.tensor_tensor(lh1f[:], lh1f[:], PH[:, 0:512], OP.mult))
                p.v(lambda e, k=k: e.tensor_scalar(lh1f[:], lh1f[:], cb1[:, k:k + 1],
                                                   mod_sb[:, 48 + k:48 + k + 1], OP.mult, OP.add))
                p.v(lambda e, dst=dst: e.tensor_copy(h2_hi[:, dst:dst + 512], lh1f[:]))
                p.v(lambda e, dst=dst: e.tensor_tensor(h2_lo[:, dst:dst + 512], lh1f[:],
                                                       h2_hi[:, dst:dst + 512], OP.subtract))
            tok_h2 = p.pub("vector")
            p.wait("tensor", tok_h2)
            if debug:
                p.wait("scalar", tok_h2)
                for k in range(ND):
                    p.v(lambda e, k=k: e.tensor_tensor(lh1f[:], h2_hi[:, k * 512:(k + 1) * 512],
                                                       h2_lo[:, k * 512:(k + 1) * 512], OP.add))
                    tok_dv = p.pub("vector")
                    p.wait("scalar", tok_dv)
                    tok_d = p.dma("scalar", dbg["h2"][k * 128:(k + 1) * 128, xoff:xoff + 512],
                                  lh1f[:])
                    p.wait("vector", tok_d)

            p.wait("tensor", tok_gw)
            for t4 in range(4):
                for k in range(ND):
                    p.mm3(PA[0:128, 0:NE],
                          h2_hi[:, k * 512 + t4 * 128: k * 512 + (t4 + 1) * 128],
                          h2_lo[:, k * 512 + t4 * 128: k * 512 + (t4 + 1) * 128],
                          gw_hi[:, k * NE:(k + 1) * NE], gw_lo[:, k * NE:(k + 1) * NE],
                          start=(k == 0), stop=(k == ND - 1))
                tok_t = p.tpub()
                p.wait("scalar", tok_t)
                p.s(lambda e, t4=t4: e.activation(sc_tm[:, t4 * NE:(t4 + 1) * NE],
                                                              PA[0:128, 0:NE], AF.Sigmoid))
                tok_s = p.pub("scalar")
                p.wait("tensor", tok_s)
            tok_s = p.pub("scalar")
            p.wait("vector", tok_s)
            p.v(lambda e: e.tensor_copy(scm[:], sc_tm[:]))
            for it, pk in enumerate((pick1, pick2)):
                p.v(lambda e: e.tensor_reduce(mrow[:],
                                              scm[:].rearrange("p (t e) -> p t e", e=NE),
                                              AX.X, OP.max))
                p.vdrain()
                fpair = (gfa, gfb)
                p.v(lambda e: e.memset(gfa[:], 0.0))
                for e_ in range(NE):
                    fc, fn_ = fpair[e_ % 2], fpair[(e_ + 1) % 2]
                    p.v(lambda e, e_=e_: e.tensor_tensor(
                        gta[:].unsqueeze(2), ecol(scm, e_), mrow[:].unsqueeze(2), OP.is_ge))
                    p.vdrain()
                    p.v(lambda e, fc=fc: e.scalar_tensor_tensor(
                        gtb[:], fc[:], -1.0, gta[:], OP.mult, OP.add))
                    p.vdrain()
                    p.v(lambda e: e.tensor_relu(gtc[:], gtb[:]))
                    p.vdrain()
                    p.v(lambda e, e_=e_, pk=pk: e.tensor_copy(ecol(pk, e_),
                                                              gtc[:].unsqueeze(2)))
                    p.v(lambda e, fc=fc, fn_=fn_: e.tensor_tensor(fn_[:], fc[:], gtc[:],
                                                                  OP.add))
                    p.vdrain()
                if it == 0:
                    p.vdrain()
                    p.v(lambda e: e.scalar_tensor_tensor(scm[:], pick1[:], NEG, sc_tm[:],
                                                         OP.mult, OP.add))
                    p.vdrain()
            p.vdrain()
            p.v(lambda e: e.tensor_tensor(gtmp[:], pick1[:], pick2[:], OP.add))
            p.vdrain()
            p.v(lambda e: e.tensor_tensor(scm[:], sc_tm[:], gtmp[:], OP.mult))
            tok_v = p.pub("vector")
            p.wait("tensor", tok_v)
            tok_m = None
            for ee in range(NE):
                pj = PC if (ee % 2 == 0) else PD
                for t4 in range(4):
                    p.transpose(pj[0:1, t4 * 128:(t4 + 1) * 128],
                                scm[:, t4 * NE + ee: t4 * NE + ee + 1], ident[:])
                tok_t = p.tpub()
                p.wait("vector", tok_t)
                p.v(lambda e, ee=ee, pj=pj: e.tensor_copy(
                    m_rows[0:1, ee * 512:(ee + 1) * 512], pj[0:1, 0:512]))
                tok_m = p.pub("vector")
                p.wait("tensor", tok_m)
            if debug:
                p.wait("scalar", tok_m)
                for ee in range(NE):
                    tok_d = p.dma("scalar", dbg["m"][ee:ee + 1, xoff:xoff + 512],
                                  m_rows[0:1, ee * 512:(ee + 1) * 512])
                p.wait("scalar", tok_d)

            for e_ in range(NE + 1):
                routed = e_ < NE
                if routed:
                    p.wait("vector", mb_free)
                    p.v(lambda e, e_=e_: e.tensor_copy(bcpad[0:1, 0:512],
                                                       m_rows[0:1, e_ * 512:(e_ + 1) * 512]))
                    tok_v = p.pub("vector")
                    p.wait("tensor", tok_v)
                    p.wait("tensor", mbank2_free)
                    tok_t = bcast_row(PG[:, 0:512], 512)
                    p.wait("vector", tok_t)
                    p.v(lambda e: e.tensor_copy(mb[:], PG[:, 0:512]))
                    tok_v = p.pub("vector")
                    mbank2_free = tok_v
                    p.wait("tensor", tok_v)
                for ft in range(8):
                    for which in range(2):
                        slot = which
                        p.wait("sync", w13_free[slot])
                        tok_w = load_w13(e_, ft, which, slot)
                        p.wait("gpsimd", tok_w)
                        p.wait("gpsimd", w13_free[slot])
                        p.g(lambda e, slot=slot: e.tensor_copy(w13_hi[slot][:], w13_ld[slot][:]))
                        p.g(lambda e, slot=slot: e.tensor_tensor(w13_lo[slot][:],
                                                                 w13_ld[slot][:],
                                                                 w13_hi[slot][:], OP.subtract))
                        tok_v = p.pub("gpsimd")
                        p.wait("tensor", tok_v)
                        if which == 0:
                            p.wait("tensor", hbank_free[ft % 2])
                        pj = (PA, PB)[ft % 2] if which == 0 else (PC, PD)[ft % 2]
                        for k in range(ND):
                            p.mm3(pj[:, 0:512],
                                  w13_hi[slot][:, k * 128:(k + 1) * 128],
                                  w13_lo[slot][:, k * 128:(k + 1) * 128],
                                  h2_hi[:, k * 512:(k + 1) * 512],
                                  h2_lo[:, k * 512:(k + 1) * 512],
                                  start=(k == 0), stop=(k == ND - 1))
                        tok_t = p.tpub()
                        w13_free[slot] = tok_t
                    p.wait("vector", tok_t)
                    p1 = (PA, PB)[ft % 2]
                    p3 = (PC, PD)[ft % 2]
                    gdst = ft * 512
                    if routed:
                        p.v(lambda e, p1=p1: e.tensor_tensor(af[:], p1[:, 0:512], mb[:],
                                                             OP.mult))
                        tok_v = p.pub("vector")
                        p.wait("scalar", tok_v)
                        p.s(lambda e: e.activation(sf[:], af[:], AF.Silu))
                        tok_s = p.pub("scalar")
                        p.wait("vector", tok_s)
                        p.v(lambda e, p3=p3: e.tensor_tensor(bf[:], p3[:, 0:512], mb[:],
                                                             OP.mult))
                        p.v(lambda e: e.tensor_tensor(af[:], sf[:], bf[:], OP.mult))
                    else:
                        tok_v = p.pub("vector")
                        p.wait("scalar", tok_v)
                        p.s(lambda e, p1=p1: e.activation(sf[:], p1[:, 0:512],
                                                                      AF.Silu))
                        tok_s = p.pub("scalar")
                        p.wait("vector", tok_s)
                        p.v(lambda e, p3=p3: e.tensor_tensor(af[:], sf[:], p3[:, 0:512],
                                                             OP.mult))
                    p.v(lambda e, gdst=gdst: e.tensor_copy(g_hi[:, gdst:gdst + 512], af[:]))
                    p.v(lambda e, gdst=gdst: e.tensor_tensor(g_lo[:, gdst:gdst + 512], af[:],
                                                             g_hi[:, gdst:gdst + 512],
                                                             OP.subtract))
                    tok_v = p.pub("vector")
                    hbank_free[ft % 2] = tok_v
                    p.wait("scalar", tok_v)
                w2_tok = [None, None]
                w2spl_free = [None, None]
                p.wait("tensor", hbank_free[0])
                p.wait("tensor", hbank_free[1])
                p.wait("sync", w2_free[0])
                w2_tok[0] = load_w2(e_, 0, 0)
                for dc in range(ND):
                    slot = dc % 2
                    if dc < ND - 1:
                        ns = (dc + 1) % 2
                        p.wait("sync", w2_free[ns])
                        w2_tok[ns] = load_w2(e_, dc + 1, ns)
                    p.wait("gpsimd", w2_tok[slot])
                    p.wait("gpsimd", w2spl_free[slot])
                    p.g(lambda e, slot=slot: e.tensor_copy(w2s_hi[slot][:], w2_ld[slot][:]))
                    p.g(lambda e, slot=slot: e.tensor_tensor(w2s_lo[slot][:], w2_ld[slot][:],
                                                             w2s_hi[slot][:], OP.subtract))
                    tok_v = p.pub("gpsimd")
                    w2_free[slot] = tok_v
                    p.wait("tensor", tok_v)
                    p.wait("tensor", rbank_free[dc % 2])
                    pj = (PE_, PF)[dc % 2]
                    for fc in range(8):
                        p.mm3(pj[:, 0:512],
                              w2s_hi[slot][:, fc * 128:(fc + 1) * 128],
                              w2s_lo[slot][:, fc * 128:(fc + 1) * 128],
                              g_hi[:, fc * 512:(fc + 1) * 512],
                              g_lo[:, fc * 512:(fc + 1) * 512],
                              start=(fc == 0), stop=(fc == 7))
                    tok_t = p.tpub()
                    w2spl_free[slot] = tok_t
                    p.wait("vector", tok_t)
                    dst = dc * TOWN + xoff
                    p.v(lambda e, dc=dc, pj=pj: e.tensor_scalar(af[:], pj[:, 0:512],
                                                                gb_cl[:, dc:dc + 1], None,
                                                                OP.mult))
                    p.v(lambda e, dst=dst: e.tensor_tensor(x1[:, dst:dst + 512],
                                                           x1[:, dst:dst + 512], af[:],
                                                           OP.add))
                    tok_v = p.pub("vector")
                    rbank_free[dc % 2] = tok_v
                    mb_free = tok_v
                hbank_free = [rbank_free[0], rbank_free[1]]
            h2_free = rbank_free[1]
        p.barrier()
        stF.close()

    p.barrier()
    last = None
    for k in range(ND):
        last = p.dma("scalar", out_d[k * 128:(k + 1) * 128, :], x1[:, k * TOWN:(k + 1) * TOWN])
    p.wait("scalar", last)
    p.wait("sync", last)

    with nc.Block() as block:
        p.emit(block)
    st.close()
    return nc


_NC_CACHE = {}


def _get_nc(debug=False, stage=None):
    key = (debug, stage if stage is not None else STAGE)
    if key not in _NC_CACHE:
        _NC_CACHE[key] = build_program(debug=debug, stage=stage)
    return _NC_CACHE[key]


def prep_inputs(inputs, c):
    b, j = c // 4, c % 4
    x = np.ascontiguousarray(np.asarray(inputs["L0_attn_x"])[b])
    others = [i for i in range(4) if i != j]
    perm = []
    for t in range(8):
        perm.append(4 * t + j)
        perm.extend(4 * t + o for o in others)
    perm = np.array(perm)
    cols = (perm[:, None] * 128 + np.arange(128)[None, :]).ravel()
    xT = np.ascontiguousarray(x.T[:, cols])

    tri = np.where(np.arange(128)[None, :] <= np.arange(128)[:, None], 0.0, NEG).astype(np.float32)
    mq = np.zeros((128, 4, 128), np.float32)
    mq[:, 0, :] = tri
    for i in range(1, 4):
        mq[:, i, :] = 0.0 if others[i - 1] < j else NEG
    return {
        "xT": xT.astype(np.float32),
        "c_fm_d": np.ascontiguousarray(np.asarray(inputs["L0_c"])[b].reshape(16, 128).T).astype(np.float32),
        "maskq": np.ascontiguousarray(mq.reshape(128, 512)),
    }


def prep_shared(inputs):
    w_qkv = np.asarray(inputs["L0_attn_w_qkv"])
    w_out = np.asarray(inputs["L0_attn_w_out"])
    w1 = np.asarray(inputs["L0_ffn_w1"])
    w2 = np.asarray(inputs["L0_ffn_w2"])
    w3 = np.asarray(inputs["L0_ffn_w3"])
    ws1 = np.asarray(inputs["L0_ffn_ws1"])
    ws2 = np.asarray(inputs["L0_ffn_ws2"])
    ws3 = np.asarray(inputs["L0_ffn_ws3"])

    woutT = w_out.T
    wout_t = np.ascontiguousarray(
        woutT.reshape(ND, 128, ND, 128).transpose(2, 1, 0, 3).reshape(ND, 128, ND * 128))
    blocks = []
    for e in range(NE):
        for w in (w1[e], w3[e]):
            blocks.append(w.T.reshape(ND, 128, FH // 128, 128).transpose(2, 1, 0, 3)
                          .reshape(FH // 128, 128, ND * 128))
    for w in (ws1, ws3):
        blocks.append(w.T.reshape(ND, 128, FH // 128, 128).transpose(2, 1, 0, 3)
                      .reshape(FH // 128, 128, ND * 128))
    w13_t = np.ascontiguousarray(np.stack(blocks, 0))
    blocks = []
    for e in range(NE):
        blocks.append(w2[e].T.reshape(FH // 128, 128, ND, 128).transpose(2, 1, 0, 3)
                      .reshape(ND, 128, FH // 128 * 128))
    blocks.append(ws2.T.reshape(FH // 128, 128, ND, 128).transpose(2, 1, 0, 3)
                  .reshape(ND, 128, FH // 128 * 128))
    w2_t = np.ascontiguousarray(np.stack(blocks, 0))

    return {
        "wmod_d": np.ascontiguousarray(np.asarray(inputs["L0_wmod"])).astype(np.float32),
        "wqkv_t": np.ascontiguousarray(
            w_qkv.T.reshape(ND, 128, 48, 128).transpose(2, 1, 0, 3)
            .reshape(48, 128, ND * 128)).astype(np.float32),
        "wout_t": wout_t.astype(np.float32),
        "w13_t": w13_t.astype(np.float32),
        "w2_t": w2_t.astype(np.float32),
        "gwT": np.ascontiguousarray(np.asarray(inputs["L0_ffn_gate_w"]).T).astype(np.float32),
        "eye": np.eye(128, dtype=np.float32),
    }


def run(inputs, debug=False, stage=None):
    nc = _get_nc(debug=debug, stage=stage)
    shared = prep_shared(inputs)
    in_maps = []
    for c in range(8):
        m = dict(shared)
        m.update(prep_inputs(inputs, c))
        in_maps.append(m)
    return run_bass_kernel_spmd(nc, in_maps, core_ids=list(range(8)))


def assemble(results, inputs):
    out = np.zeros((2, S, D), np.float32)
    for c in range(8):
        b, j = c // 4, c % 4
        o = results[c]["out"]
        rows = (np.array([j + 4 * mm for mm in range(8)])[:, None] * 128
                + np.arange(128)[None, :]).ravel()
        out[b, rows, :] = o.T
    return out


def kernel(**inputs):
    res = run(inputs, debug=False)
    return assemble(res.results, inputs)


# version 17 (final)
# speedup vs baseline: 1.0308x; 1.0308x over previous
import numpy as np
from contextlib import ExitStack

import concourse.bass as bass
import concourse.mybir as mybir
from concourse.bass_utils import run_bass_kernel_spmd

F32 = mybir.dt.float32
BF16 = mybir.dt.bfloat16
AF = mybir.ActivationFunctionType
OP = mybir.AluOpType
AX = mybir.AxisListType

D = 2048
S = 4096
H = 16
HD = 128
NE = 8
FH = 1024
ND = D // 128
TOWN = 1024
NEG = -1.0e30
EPS = 1e-5
QSCALE = float(HD) ** -0.5

STAGE = 99
DEBUG = False
M_EB = 24
M_T = True

ENGS = ("sync", "tensor", "vector", "scalar", "gpsimd")


class Chan:
    def __init__(self, nc, st, name):
        self.h = st.enter_context(nc.semaphore(name))
        self.n = 0


class Plan:
    def __init__(self, nc, st):
        self.nc = nc
        self.ops = {e: [] for e in ENGS}
        self.ch = {name: Chan(nc, st, "ch_" + name)
                   for name in ("ld", "d0", "d1", "d2", "d3", "d4", "d5", "gd", "sd",
                                "T", "V", "S", "G")}

    def raw(self, eng, fn):
        self.ops[eng].append(fn)

    def v(self, fn):
        self.ops["vector"].append(fn)
        self.ops["vector"].append(lambda e: e.drain())

    def s(self, fn):
        self.ops["scalar"].append(fn)
        self.ops["scalar"].append(lambda e: e.drain())

    def g(self, fn):
        self.ops["gpsimd"].append(fn)
        self.ops["gpsimd"].append(lambda e: e.drain())

    def vdrain(self):
        self.ops["vector"].append(lambda e: e.drain())

    def sdrain(self):
        self.ops["scalar"].append(lambda e: e.drain())

    def dma(self, q, out, in_, ch=None):
        ch = ch or {"sync": "ld", "gpsimd": "gd", "scalar": "sd"}[q]
        c = self.ch[ch]
        c.n += 16
        v = c.n
        self.raw(q, lambda e, out=out, in_=in_, h=c.h:
                 e.dma_start(out=out, in_=in_).then_inc(h, 16))
        return (ch, v)

    def mm(self, psum, lhsT, rhs, start, stop, inc=False):
        if inc:
            c = self.ch["T"]
            c.n += 1
            v = c.n
            self.raw("tensor", lambda e, o=psum, l=lhsT, r=rhs, st_=start, sp=stop, h=c.h:
                     e.matmul(o, l, r, start=st_, stop=sp).then_inc(h, 1))
            return ("T", v)
        self.raw("tensor", lambda e, o=psum, l=lhsT, r=rhs, st_=start, sp=stop:
                 e.matmul(o, l, r, start=st_, stop=sp))
        return None

    def mm3(self, psum, lh, ll, rh, rl, start, stop, inc=False):
        self.mm(psum, lh, rh, start, False)
        self.mm(psum, lh, rl, False, False)
        return self.mm(psum, ll, rh, False, stop, inc=inc)

    def transpose(self, psum, in_, ident, inc=False):
        if inc:
            c = self.ch["T"]
            c.n += 1
            v = c.n
            self.raw("tensor", lambda e, o=psum, i=in_, d=ident, h=c.h:
                     e.transpose(o, i, d).then_inc(h, 1))
            return ("T", v)
        self.raw("tensor", lambda e, o=psum, i=in_, d=ident: e.transpose(o, i, d))
        return None

    def pub(self, eng):
        key = {"vector": "V", "scalar": "S", "gpsimd": "G", "tensor": "T"}[eng]
        c = self.ch[key]
        c.n += 1
        v = c.n
        self.raw(eng, lambda e, h=c.h: e.drain().then_inc(h, 1))
        return (key, v)

    def tpub(self):
        return self.pub("tensor")

    DMA_CH = ("ld", "d0", "d1", "d2", "d3", "d4", "d5", "gd", "sd")

    def wait(self, eng, tok):
        if tok is None:
            return
        ch, v = tok
        if ch in self.DMA_CH:
            v = self.ch[ch].n
        self.raw(eng, lambda e, h=self.ch[ch].h, v=v: e.wait_ge(h, v))

    def barrier(self):
        toks = [self.pub(e) for e in ("vector", "scalar", "gpsimd", "tensor")]
        snap = [(k, self.ch[k].n) for k in self.DMA_CH if self.ch[k].n]
        for eng in ENGS:
            for t in toks:
                self.wait(eng, t)
            for kv in snap:
                self.wait(eng, kv)

    def emit(self, block):
        for eng in ENGS:
            dec = getattr(block, eng)

            def mk(eng_name):
                def body(e):
                    for fn in self.ops[eng_name]:
                        fn(e)
                return body
            dec(mk(eng))


def build_program(debug=False, stage=None):
    stage = STAGE if stage is None else stage
    nc = bass.Bass("TRN2", target_bir_lowering=False)
    st = ExitStack()

    def dram_in(name, shape, dt=F32):
        return nc.dram_tensor(name, shape, dt, kind="ExternalInput").ap()

    def dram_out(name, shape, dt=F32):
        return nc.dram_tensor(name, shape, dt, kind="ExternalOutput").ap()

    def dram_tmp(name, shape, dt):
        return nc.dram_tensor(name, shape, dt, kind="Internal").ap()

    def sb(name, shape, dt, stack=None):
        return (stack or st).enter_context(nc.sbuf_tensor(name, shape, dt))

    def psum(name, shape=(128, 512)):
        return st.enter_context(nc.psum_tensor(name, list(shape), F32))

    xT = dram_in("xT", [D, S])
    wmod_d = dram_in("wmod_d", [D, 6 * D])
    c_fm_d = dram_in("c_fm_d", [128, 16])
    wqkv_t = dram_in("wqkv_t", [48, 128, ND * 128])
    wout_t = dram_in("wout_t", [ND, 128, ND * 128])
    w13_t = dram_in("w13_t", [2 * NE + 2, FH // 128, 128, ND * 128])
    w2_t = dram_in("w2_t", [NE + 1, ND, 128, 8 * 128])
    gwT = dram_in("gwT", [D, NE])
    maskq = dram_in("maskq", [128, 512])
    eye = dram_in("eye", [128, 128])
    out_d = dram_out("out", [D, TOWN])

    dbg = {}
    if debug:
        for nm, shp in [("mod", [128, 96]), ("h1own", [D, TOWN]), ("kc", [128, 512]),
                        ("vt", [128, 128]), ("q", [128, TOWN]), ("y", [D, TOWN]),
                        ("x1", [D, TOWN]), ("h2", [D, TOWN]), ("m", [NE, TOWN])]:
            dbg[nm] = dram_out("dbg_" + nm, shp)

    s_w_hi = dram_tmp("s_w_hi", [48, 128, ND * 128], BF16)
    s_w_lo = dram_tmp("s_w_lo", [48, 128, ND * 128], BF16)
    s_k_hi = dram_tmp("s_k_hi", [H, 128, 8 * 512], BF16)
    s_k_lo = dram_tmp("s_k_lo", [H, 128, 8 * 512], BF16)
    s_v_hi = dram_tmp("s_v_hi", [H, 128, 32 * 128], BF16)
    s_v_lo = dram_tmp("s_v_lo", [H, 128, 32 * 128], BF16)
    s_h1_hi = dram_tmp("s_h1_hi", [ND, 128, TOWN], BF16)
    s_h1_lo = dram_tmp("s_h1_lo", [ND, 128, TOWN], BF16)
    s_q_hi = dram_tmp("s_q_hi", [H, 128, TOWN], BF16)
    s_q_lo = dram_tmp("s_q_lo", [H, 128, TOWN], BF16)

    p = Plan(nc, st)

    PA, PB = psum("PA"), psum("PB")
    PC, PD = psum("PC"), psum("PD")
    PE_, PF = psum("PE"), psum("PF")
    PG, PH = psum("PG"), psum("PH")

    ones_m1 = sb("ones_m1", [128, 1], BF16)
    ones_k1 = sb("ones_k1", [32, 128], F32)
    bcpad = sb("bcpad", [32, 512], F32)
    ident = sb("ident", [128, 128], F32)
    ident_b = sb("ident_b", [128, 128], BF16)
    mask_sb = sb("mask_sb", [128, 512], F32)
    warm = sb("warm", [128, 128], BF16)
    mod_sb = sb("mod_sb", [128, 96], F32)
    ca1 = sb("ca1", [128, 16], F32)
    cb1 = sb("cb1", [128, 16], F32)
    ga_cl = sb("ga_cl", [128, 16], F32)
    gb_cl = sb("gb_cl", [128, 16], F32)
    gw_hi = sb("gw_hi", [128, 16 * NE], BF16)
    gw_lo = sb("gw_lo", [128, 16 * NE], BF16)
    c_fm = sb("c_fm", [128, 16], F32)
    s_fm = sb("s_fm", [128, 16], F32)
    col_a = sb("col_a", [128, 1], F32)
    col_b = sb("col_b", [128, 1], F32)
    col_c = sb("col_c", [128, 1], F32)
    col_d = sb("col_d", [128, 1], F32)
    col_e = sb("col_e", [128, 1], F32)
    eps_col = sb("eps_col", [1, 1], F32)

    t_eye = p.dma("sync", ident[:], eye[:, :])
    p.dma("sync", mask_sb[:], maskq[:, :])
    t_crow = p.dma("sync", c_fm[:], c_fm_d[:, :])
    p.wait("vector", t_eye)
    p.v(lambda e: e.tensor_copy(ident_b[:], ident[:]))

    p.v(lambda e: e.memset(ones_m1[:], 1.0))
    p.v(lambda e: e.memset(ones_k1[:], 0.0))
    p.vdrain()
    p.v(lambda e: e.memset(ones_k1[0:1, :], 1.0))
    p.v(lambda e: e.memset(bcpad[:], 0.0))
    p.v(lambda e: e.memset(warm[:], 0.0))
    p.v(lambda e: e.memset(eps_col[:], EPS))
    tok_init_v = p.pub("vector")

    gw_f = sb("gw_f", [128, 16 * NE], F32)
    gwT_r = gwT.rearrange("(c p) e -> c p e", p=128)
    tks = [p.dma("sync", gw_f[:, k * NE:(k + 1) * NE], gwT_r[k], ch="d0") for k in range(ND)]
    p.wait("vector", tks[-1])
    p.v(lambda e: e.tensor_copy(gw_hi[:], gw_f[:]))
    p.vdrain()
    p.v(lambda e: e.tensor_tensor(gw_lo[:], gw_f[:], gw_hi[:], OP.subtract))
    tok_gw = p.pub("vector")

    p.wait("tensor", tok_init_v)
    p.mm(PA[0:128, 0:128], warm[:], warm[:], True, True)

    def bcast_row(psum_view, n):
        return p.mm(psum_view, ones_k1[:], bcpad[0:32, 0:n], True, True, inc=True)

    stM = ExitStack()
    mod_row = sb("mod_row", [1, 6 * D], F32, stM)
    wm_ld = [sb(f"wmld{i}", [128, 16 * 512], F32, stM) for i in range(2)]
    mscr = [sb(f"mscr{i}", [128, 512], F32, stM) for i in range(2)]

    p.wait("scalar", t_crow)
    p.s(lambda e: e.activation(s_fm[:], c_fm[:], AF.Silu))
    tok_silu = p.pub("scalar")
    p.wait("tensor", tok_silu)

    def load_wm(eb, slot):
        tk = None
        for dc in range(ND):
            tk = p.dma("sync", wm_ld[slot][:, dc * 512:(dc + 1) * 512],
                       wmod_d[dc * 128:(dc + 1) * 128, eb * 512:(eb + 1) * 512],
                       ch=("d2", "d3")[slot])
        return tk

    wm_tok = [load_wm(0, 0), None]
    wm_free = [None, None]
    mbank_free = [None, None]
    for eb in range(M_EB):
        slot = eb % 2
        if eb < M_EB - 1:
            ns = (eb + 1) % 2
            p.wait("sync", wm_free[ns])
            wm_tok[ns] = load_wm(eb + 1, ns)
        p.wait("tensor", wm_tok[slot])
        p.wait("tensor", mbank_free[eb % 2])
        pj = PG if (eb % 2 == 0) else PH
        for dc in range(ND):
            p.mm(pj[0:1, 0:512], s_fm[:, dc:dc + 1],
                 wm_ld[slot][:, dc * 512:(dc + 1) * 512],
                 start=(dc == 0), stop=(dc == ND - 1))
        tok_t = p.tpub()
        wm_free[slot] = tok_t
        p.wait("scalar", tok_t)
        p.s(lambda e, pj=pj, eb=eb: e.activation(
            mod_row[0:1, eb * 512:(eb + 1) * 512], pj[0:1, 0:512], AF.Copy))
        tok_s = p.pub("scalar")
        mbank_free[eb % 2] = tok_s
    tok_s_final = p.pub("scalar")
    p.wait("tensor", tok_s_final)
    for half in range(2 if M_T else 0):
        for r in range(48):
            rr = half * 48 + r
            p.transpose(PG[:, r:r + 1],
                        mod_row[0:1, rr * 128:(rr + 1) * 128], ident[0:1, 0:1])
        tok_t = p.tpub()
        p.wait("vector", tok_t)
        p.v(lambda e, half=half: e.tensor_copy(mod_sb[:, half * 48:(half + 1) * 48],
                                               PG[:, 0:48]))
        tok_v = p.pub("vector")
        p.wait("tensor", tok_v)
    p.vdrain()
    p.v(lambda e: e.tensor_scalar(ca1[:], mod_sb[:, 16:32], 1.0, None, OP.add))
    p.v(lambda e: e.tensor_scalar(cb1[:], mod_sb[:, 64:80], 1.0, None, OP.add))
    tok_v = p.pub("vector")
    p.wait("scalar", tok_v)

    def clamp_gate(dst, lo):
        p.v(lambda e: e.tensor_scalar(mscr[0][:, 16:32], mod_sb[:, lo:lo + 16], 0.0, None,
                                      OP.is_ge))
        p.vdrain()
        p.v(lambda e: e.tensor_scalar(mscr[0][:, 32:48], mscr[0][:, 16:32], 2.0, -1.0,
                                      OP.mult, OP.add))
        tok_vv = p.pub("vector")
        p.wait("scalar", tok_vv)
        p.s(lambda e: e.activation(mscr[0][:, 0:16], mod_sb[:, lo:lo + 16], AF.Abs))
        tok_s = p.pub("scalar")
        p.wait("vector", tok_s)
        p.v(lambda e: e.tensor_scalar(mscr[1][:, 0:16], mscr[0][:, 0:16], 0.25, None, OP.max))
        p.vdrain()
        p.v(lambda e: e.tensor_tensor(dst[:], mscr[0][:, 32:48], mscr[1][:, 0:16], OP.mult))

    clamp_gate(ga_cl, 32)
    clamp_gate(gb_cl, 80)
    tok_mod = p.pub("vector")
    if debug:
        p.wait("scalar", tok_mod)
        p.dma("scalar", dbg["mod"][:, :], mod_sb[:])

    stW = ExitStack()
    w_ld = [sb(f"wld{i}", [128, ND * 128], F32, stW) for i in range(3)]
    w_hi2 = [sb(f"wsplh{i}", [128, ND * 128], BF16, stW) for i in range(3)]
    w_lo2 = [sb(f"wspll{i}", [128, ND * 128], BF16, stW) for i in range(3)]
    p.wait("sync", tok_gw)
    wslot_free = [None, None, None]
    wst_free = [None, None, None]
    WLCH = ("d0", "d1", "ld")
    w_tok = [p.dma("sync", w_ld[i][:], wqkv_t[i], ch=WLCH[i]) for i in range(2)] + [None]
    for sl in range(48):
        slot = sl % 3
        if sl + 2 < 48:
            ns = (sl + 2) % 3
            p.wait("sync", wslot_free[ns])
            w_tok[ns] = p.dma("sync", w_ld[ns][:], wqkv_t[sl + 2], ch=WLCH[ns])
        p.wait("gpsimd", w_tok[slot])
        p.wait("gpsimd", wst_free[slot])
        p.g(lambda e, slot=slot: e.tensor_copy(w_hi2[slot][:], w_ld[slot][:]))
        p.g(lambda e, slot=slot: e.tensor_tensor(w_lo2[slot][:], w_ld[slot][:],
                                                 w_hi2[slot][:], OP.subtract))
        tok_sp = p.pub("gpsimd")
        wslot_free[slot] = tok_sp
        p.wait("sync", tok_sp)
        stch = ("d4", "d5", "d4")[slot]
        p.dma("sync", s_w_hi[sl], w_hi2[slot][:], ch=stch)
        wst_free[slot] = p.dma("sync", s_w_lo[sl], w_lo2[slot][:], ch=stch)

    p.barrier()
    stW.close()
    stM.close()

    if stage >= 2:
        stA = ExitStack()
        xbuf = sb("xbuf", [128, ND * 512], F32, stA)
        xs_hi = sb("xs_hi", [128, 512], BF16, stA)
        xs_lo = sb("xs_lo", [128, 512], BF16, stA)
        x2f = sb("x2f", [128, 512], F32, stA)
        x2_hi = sb("x2_hi", [128, 512], BF16, stA)
        x2_lo = sb("x2_lo", [128, 512], BF16, stA)
        h1f = sb("h1f", [128, 512], F32, stA)
        h1b_hi = [sb(f"h1hi{i}", [128, ND * 512], BF16, stA) for i in range(2)]
        h1b_lo = [sb(f"h1lo{i}", [128, ND * 512], BF16, stA) for i in range(2)]
        kvw_hi = [sb(f"kvwh{i}", [128, ND * 128], BF16, stA) for i in range(2)]
        kvw_lo = [sb(f"kvwl{i}", [128, ND * 128], BF16, stA) for i in range(2)]
        evb_hi = [sb(f"evh{i}", [128, 512], BF16, stA) for i in range(2)]
        evb_lo = [sb(f"evl{i}", [128, 512], BF16, stA) for i in range(2)]
        tvb_hi = [sb(f"tvh{i}", [128, 512], BF16, stA) for i in range(2)]
        tvb_lo = [sb(f"tvl{i}", [128, 512], BF16, stA) for i in range(2)]
        mean_row = sb("mean_row", [1, 512], F32, stA)
        rstd_row = sb("rstd_row", [1, 512], F32, stA)
        var_row = sb("var_row", [1, 512], F32, stA)
        std_row = sb("std_row", [1, 512], F32, stA)

        x_tiles = xT.rearrange("(c p) s -> c p s", p=128)

        def load_x(tt):
            tk = None
            for k in range(ND):
                tk = p.dma("sync", xbuf[:, k * 512:(k + 1) * 512],
                           x_tiles[k][:, tt * 512:(tt + 1) * 512], ch="d0")
            return tk

        def load_kvw(sl, slot):
            p.dma("sync", kvw_hi[slot][:], s_w_hi[16 + sl], ch=("d2", "d3")[slot])
            return p.dma("sync", kvw_lo[slot][:], s_w_lo[16 + sl], ch=("d2", "d3")[slot])

        st_A = {
            "tok_x": load_x(0),
            "kvw_tok": [None, None],
            "kvw_free": [None, None],
            "kv_done": [None, None],
            "g_done": [None, None],
            "h1_tok": [None, None],
            "h1_read": None,
            "ev_free": [None, None],
            "tv_free": [None, None],
            "bank_free": [None, None],
            "pcd_free": None,
            "tpose_pend": None,
        }

        def front_steps(tt):
            par = tt % 2
            p.wait("vector", st_A["tok_x"])
            p.wait("scalar", st_A["tok_x"])
            tok_stats = None
            tok_v2 = None
            p.wait("tensor", st_A["h1_read"])
            for k in range(ND):
                xc = xbuf[:, k * 512:(k + 1) * 512]
                p.v(lambda e, xc=xc: e.tensor_copy(xs_hi[:], xc))
                p.v(lambda e, xc=xc: e.tensor_tensor(xs_lo[:], xc, xs_hi[:], OP.subtract))
                p.wait("scalar", tok_v2)
                p.s(lambda e, xc=xc: e.activation(x2f[:], xc, AF.Square))
                tok_s = p.pub("scalar")
                p.wait("vector", tok_s)
                p.v(lambda e: e.tensor_copy(x2_hi[:], x2f[:]))
                p.v(lambda e: e.tensor_tensor(x2_lo[:], x2f[:], x2_hi[:], OP.subtract))
                tok_v2 = p.pub("vector")
                p.wait("tensor", tok_v2)
                p.mm(PG[0:1, 0:512], ones_m1[:], xs_hi[:], k == 0, False)
                p.mm(PG[0:1, 0:512], ones_m1[:], xs_lo[:], False, k == ND - 1)
                p.mm(PH[0:1, 0:512], ones_m1[:], x2_hi[:], k == 0, False)
                tok_stats = p.mm(PH[0:1, 0:512], ones_m1[:], x2_lo[:], False, k == ND - 1,
                                 inc=(k == ND - 1))
                tok_t2 = p.tpub()
                p.wait("vector", tok_t2)
                yield
            p.wait("vector", tok_stats)
            p.v(lambda e: e.tensor_scalar(mean_row[:], PG[0:1, 0:512], 1.0 / D, None, OP.mult))
            p.v(lambda e: e.tensor_scalar(var_row[:], PH[0:1, 0:512], 1.0 / D, None, OP.mult))
            p.vdrain()
            p.v(lambda e: e.tensor_tensor(x2f[0:1, 0:512], mean_row[:], mean_row[:], OP.mult))
            p.vdrain()
            p.v(lambda e: e.tensor_tensor(var_row[:], var_row[:], x2f[0:1, 0:512], OP.subtract))
            tok_v = p.pub("vector")
            p.wait("scalar", tok_v)
            p.s(lambda e: e.activation(std_row[:], var_row[:], AF.Sqrt,
                                       bias=eps_col[0:1, 0:1]))
            tok_s = p.pub("scalar")
            p.wait("vector", tok_s)
            p.v(lambda e: e.reciprocal(rstd_row[:], std_row[:]))
            p.v(lambda e: e.tensor_copy(bcpad[0:1, 0:512], mean_row[:]))
            tok_v = p.pub("vector")
            p.wait("tensor", tok_v)
            bcast_row(PG[:, 0:512], 512)
            tok_t = p.tpub()
            p.wait("vector", tok_t)
            p.v(lambda e: e.tensor_copy(bcpad[0:1, 0:512], rstd_row[:]))
            tok_v = p.pub("vector")
            p.wait("tensor", tok_v)
            tok_t = bcast_row(PH[:, 0:512], 512)
            p.wait("vector", tok_t)
            hh, hl = h1b_hi[par], h1b_lo[par]
            p.wait("vector", tok_mod)
            p.wait("vector", st_A["kv_done"][par])
            p.wait("vector", st_A["g_done"][par])
            for k in range(ND):
                xc = xbuf[:, k * 512:(k + 1) * 512]
                p.v(lambda e, xc=xc: e.tensor_tensor(h1f[:], xc, PG[:, 0:512], OP.subtract))
                p.v(lambda e: e.tensor_tensor(h1f[:], h1f[:], PH[:, 0:512], OP.mult))
                p.v(lambda e, k=k: e.tensor_scalar(h1f[:], h1f[:], ca1[:, k:k + 1],
                                                   mod_sb[:, k:k + 1], OP.mult, OP.add))
                p.v(lambda e, k=k, hh=hh: e.tensor_copy(hh[:, k * 512:(k + 1) * 512], h1f[:]))
                p.v(lambda e, k=k, hh=hh, hl=hl: e.tensor_tensor(
                    hl[:, k * 512:(k + 1) * 512], h1f[:], hh[:, k * 512:(k + 1) * 512],
                    OP.subtract))
                yield
            tok_h1 = p.pub("vector")
            st_A["h1_tok"][par] = tok_h1
            st_A["h1_read"] = tok_h1
            if tt < 7:
                p.wait("sync", tok_h1)
                st_A["tok_x"] = load_x(tt + 1)
            p.wait("gpsimd", tok_h1)
            gd = None
            for k in range(ND):
                p.dma("gpsimd", s_h1_hi[k][:, tt * 128:(tt + 1) * 128],
                      hh[:, k * 512:k * 512 + 128])
                gd = p.dma("gpsimd", s_h1_lo[k][:, tt * 128:(tt + 1) * 128],
                           hl[:, k * 512:k * 512 + 128])
            st_A["g_done"][par] = gd

        def flush_tpose():
            pend = st_A["tpose_pend"]
            if pend is None:
                return
            eslot, h_, tt_ = pend
            st_A["tpose_pend"] = None
            p.wait("tensor", st_A["tv_free"][eslot])
            p.wait("tensor", st_A["pcd_free"])
            for q4 in range(4):
                p.transpose(PC[:, q4 * 64:(q4 + 1) * 64].bitcast(BF16),
                            evb_hi[eslot][:, q4 * 128:(q4 + 1) * 128], ident_b[:])
            for q4 in range(4):
                p.transpose(PD[:, q4 * 64:(q4 + 1) * 64].bitcast(BF16),
                            evb_lo[eslot][:, q4 * 128:(q4 + 1) * 128], ident_b[:])
            tok_t2 = p.tpub()
            st_A["ev_free"][eslot] = tok_t2
            p.wait("vector", tok_t2)
            tv_h, tv_l = tvb_hi[eslot], tvb_lo[eslot]
            p.v(lambda e, tv_h=tv_h: e.tensor_copy(tv_h[:], PC[:, 0:256].bitcast(BF16)))
            p.v(lambda e, tv_l=tv_l: e.tensor_copy(tv_l[:], PD[:, 0:256].bitcast(BF16)))
            tok_v2 = p.pub("vector")
            st_A["pcd_free"] = tok_v2
            p.wait("gpsimd", tok_v2)
            p.dma("gpsimd", s_v_hi[h_][:, tt_ * 512:(tt_ + 1) * 512], tv_h[:])
            tok_g = p.dma("gpsimd", s_v_lo[h_][:, tt_ * 512:(tt_ + 1) * 512], tv_l[:])
            st_A["tv_free"][eslot] = tok_g

        def kv_steps(tt):
            par = tt % 2
            hh, hl = h1b_hi[par], h1b_lo[par]
            p.wait("tensor", st_A["h1_tok"][par])
            if st_A["kvw_tok"][0] is None:
                st_A["kvw_tok"][0] = load_kvw(0, 0)
            kv_last = None
            for sl in range(32):
                wslot = sl % 2
                if sl < 31:
                    ns = (sl + 1) % 2
                    p.wait("sync", st_A["kvw_free"][ns])
                    st_A["kvw_tok"][ns] = load_kvw(sl + 1, ns)
                elif tt < 7:
                    ns = (sl + 1) % 2
                    p.wait("sync", st_A["kvw_free"][ns])
                    st_A["kvw_tok"][ns] = load_kvw(0, ns)
                p.wait("tensor", st_A["kvw_tok"][wslot])
                p.wait("tensor", st_A["bank_free"][sl % 2])
                pj = PA if (sl % 2 == 0) else PB
                for k in range(ND):
                    p.mm3(pj[:, 0:512],
                          kvw_hi[wslot][:, k * 128:(k + 1) * 128],
                          kvw_lo[wslot][:, k * 128:(k + 1) * 128],
                          hh[:, k * 512:(k + 1) * 512], hl[:, k * 512:(k + 1) * 512],
                          start=(k == 0), stop=(k == ND - 1))
                tok_t = p.tpub()
                st_A["kvw_free"][wslot] = tok_t
                kv_last = tok_t
                flush_tpose()
                p.wait("vector", tok_t)
                eslot = sl % 2
                if sl < 16:
                    h_ = sl
                    p.wait("vector", st_A["ev_free"][eslot])
                    ev_h, ev_l = evb_hi[eslot], evb_lo[eslot]
                    p.v(lambda e, pj=pj, ev_h=ev_h: e.tensor_copy(ev_h[:], pj[:, 0:512]))
                    p.v(lambda e, pj=pj, ev_h=ev_h, ev_l=ev_l: e.tensor_tensor(
                        ev_l[:], pj[:, 0:512], ev_h[:], OP.subtract))
                    tok_v = p.pub("vector")
                    st_A["bank_free"][sl % 2] = tok_v
                    p.wait("gpsimd", tok_v)
                    p.dma("gpsimd", s_k_hi[h_][:, tt * 512:(tt + 1) * 512], ev_h[:])
                    tok_g = p.dma("gpsimd", s_k_lo[h_][:, tt * 512:(tt + 1) * 512], ev_l[:])
                    st_A["ev_free"][eslot] = tok_g
                else:
                    h_ = sl - 16
                    p.wait("vector", st_A["ev_free"][eslot])
                    ev_h, ev_l = evb_hi[eslot], evb_lo[eslot]
                    p.v(lambda e, pj=pj, ev_h=ev_h: e.tensor_copy(ev_h[:], pj[:, 0:512]))
                    p.v(lambda e, pj=pj, ev_h=ev_h, ev_l=ev_l: e.tensor_tensor(
                        ev_l[:], pj[:, 0:512], ev_h[:], OP.subtract))
                    tok_v = p.pub("vector")
                    st_A["bank_free"][sl % 2] = tok_v
                    st_A["ev_free"][eslot] = tok_v
                    p.wait("tensor", tok_v)
                    st_A["tpose_pend"] = (eslot, h_, tt)
                yield
            flush_tpose()
            st_A["kv_done"][par] = kv_last

        def drive(*gens):
            gens = [g for g in gens if g is not None]
            while gens:
                nxt = []
                for g_ in gens:
                    try:
                        next(g_)
                        nxt.append(g_)
                    except StopIteration:
                        pass
                gens = nxt

        for tt in range(8):
            drive(front_steps(tt), kv_steps(tt - 1) if tt >= 1 else None)
        drive(kv_steps(7))
        p.barrier()
        stA.close()

        if debug:
            stD = ExitStack()
            hb = sb("hb_dbg", [128, TOWN], BF16, stD)
            hb2 = sb("hb2_dbg", [128, TOWN], BF16, stD)
            hf = sb("hf_dbg", [128, TOWN], F32, stD)
            hb_free = None
            hb_free2 = None
            for k in range(ND):
                p.wait("sync", hb_free)
                t1 = p.dma("sync", hb[:], s_h1_hi[k])
                t2 = p.dma("sync", hb2[:], s_h1_lo[k])
                p.wait("vector", t2)
                p.wait("vector", hb_free2)
                p.v(lambda e: e.tensor_tensor(hf[:], hb[:], hb2[:], OP.add))
                tok_v = p.pub("vector")
                hb_free = tok_v
                p.wait("scalar", tok_v)
                tok_d = p.dma("scalar", dbg["h1own"][k * 128:(k + 1) * 128, :], hf[:])
                p.wait("scalar", tok_d)
                hb_free2 = tok_d
            p.barrier()
            stD.close()

    if stage >= 3:
        stQ = ExitStack()
        h1o_hi = sb("h1o_hi", [128, ND * TOWN], BF16, stQ)
        h1o_lo = sb("h1o_lo", [128, ND * TOWN], BF16, stQ)
        qw_hi = [sb(f"qwh{i}", [128, ND * 128], BF16, stQ) for i in range(2)]
        qw_lo = [sb(f"qwl{i}", [128, ND * 128], BF16, stQ) for i in range(2)]
        qf = sb("qf", [128, 512], F32, stQ)
        q_hi = sb("q_hi", [128, 512], BF16, stQ)
        q_lo = sb("q_lo", [128, 512], BF16, stQ)
        tk = None
        for k in range(ND):
            p.dma("sync", h1o_hi[:, k * TOWN:(k + 1) * TOWN], s_h1_hi[k])
            tk = p.dma("sync", h1o_lo[:, k * TOWN:(k + 1) * TOWN], s_h1_lo[k])
        p.wait("tensor", tk)

        def load_qw(h_, slot):
            p.dma("sync", qw_hi[slot][:], s_w_hi[h_], ch=("d0", "d1")[slot])
            return p.dma("sync", qw_lo[slot][:], s_w_lo[h_], ch=("d0", "d1")[slot])

        qw_tok = [load_qw(0, 0), None]
        qw_free = [None, None]
        qsp_free = None
        bank_free = [None, None]
        for h_ in range(H):
            slot = h_ % 2
            if h_ < H - 1:
                ns = (h_ + 1) % 2
                p.wait("sync", qw_free[ns])
                qw_tok[ns] = load_qw(h_ + 1, ns)
            p.wait("tensor", qw_tok[slot])
            for qc in range(2):
                p.wait("tensor", bank_free[qc])
                pj = PA if qc == 0 else PB
                for k in range(ND):
                    p.mm3(pj[:, 0:512],
                          qw_hi[slot][:, k * 128:(k + 1) * 128],
                          qw_lo[slot][:, k * 128:(k + 1) * 128],
                          h1o_hi[:, k * TOWN + qc * 512: k * TOWN + (qc + 1) * 512],
                          h1o_lo[:, k * TOWN + qc * 512: k * TOWN + (qc + 1) * 512],
                          start=(k == 0), stop=(k == ND - 1))
                tok_t = p.tpub()
                qw_free[slot] = tok_t
                p.wait("scalar", tok_t)
                p.wait("scalar", qsp_free)
                p.s(lambda e, pj=pj: e.activation(qf[:], pj[:, 0:512], AF.Copy,
                                                              scale=QSCALE, bias=0.0))
                tok_s = p.pub("scalar")
                bank_free[qc] = tok_s
                p.wait("vector", tok_s)
                p.v(lambda e: e.tensor_copy(q_hi[:], qf[:]))
                p.v(lambda e: e.tensor_tensor(q_lo[:], qf[:], q_hi[:], OP.subtract))
                tok_v = p.pub("vector")
                p.wait("gpsimd", tok_v)
                p.dma("gpsimd", s_q_hi[h_][:, qc * 512:(qc + 1) * 512], q_hi[:])
                tok_g = p.dma("gpsimd", s_q_lo[h_][:, qc * 512:(qc + 1) * 512], q_lo[:])
                qsp_free = tok_g
                p.wait("vector", tok_g)
                if debug and h_ == 1:
                    p.wait("scalar", tok_g)
                    tok_d = p.dma("scalar", dbg["q"][:, qc * 512:(qc + 1) * 512], qf[:])
                    p.wait("scalar", tok_d)
        p.barrier()
        stQ.close()

    stBO = ExitStack()
    y_hi = sb("y_hi", [128, H * TOWN], BF16, stBO)
    y_lo = sb("y_lo", [128, H * TOWN], BF16, stBO)

    if stage >= 4:
        stB = ExitStack()
        qb_hi = [sb(f"qbh{i}", [128, TOWN], BF16, stB) for i in range(2)]
        qb_lo = [sb(f"qbl{i}", [128, TOWN], BF16, stB) for i in range(2)]
        kb_hi = [sb(f"kbh{i}", [128, 8 * 512], BF16, stB) for i in range(2)]
        kb_lo = [sb(f"kbl{i}", [128, 8 * 512], BF16, stB) for i in range(2)]
        vb_hi = sb("vbh", [128, 32 * 128], BF16, stB)
        vb_lo = sb("vbl", [128, 32 * 128], BF16, stB)
        scpn = [sb(f"scpn{i}", [128, 8 * 512], F32, stB) for i in range(2)]
        ptb_hi = [sb(f"pth{i}", [128, 8 * 512], BF16, stB) for i in range(2)]
        ptb_lo = [sb(f"ptl{i}", [128, 8 * 512], BF16, stB) for i in range(2)]
        yf_ = sb("yf_", [128, 128], F32, stB)
        yg_ = sb("yg_", [128, 128], F32, stB)
        mxc = [sb(f"mxc{i}", [128, 1], F32, stB) for i in range(2)]
        ngc = [sb(f"ngc{i}", [128, 1], F32, stB) for i in range(2)]
        dnc = [sb(f"dnc{i}", [128, 1], F32, stB) for i in range(2)]
        rcc = [sb(f"rcc{i}", [128, 1], F32, stB) for i in range(2)]

        def load_qk(h_, slot):
            c = ("d0", "d1")[slot]
            p.dma("sync", qb_hi[slot][:], s_q_hi[h_], ch=c)
            p.dma("sync", qb_lo[slot][:], s_q_lo[h_], ch=c)
            p.dma("sync", kb_hi[slot][:], s_k_hi[h_], ch=c)
            return p.dma("sync", kb_lo[slot][:], s_k_lo[h_], ch=c)

        def load_v(h_):
            p.dma("sync", vb_hi[:], s_v_hi[h_], ch="d2")
            return p.dma("sync", vb_lo[:], s_v_lo[h_], ch="d2")

        stb = {
            "qk_tok": [load_qk(0, 0), None],
            "qk_free": [None, None],
            "v_tok": load_v(0),
            "v_free": None,
            "sbank": [None, None],
            "tbank": [None, None],
            "ybank": [None, None],
            "scpn_free": [None, None],
            "pt_free": [None, None],
            "exp_tok": [None, None],
            "rcp_tok": [None, None],
            "dbank": None,
            "bcp": None,
        }

        def sm_steps(h_, m, slot):
            par = m % 2
            W = m + 1
            qh_, ql_ = qb_hi[slot], qb_lo[slot]
            kh_, kl_ = kb_hi[slot], kb_lo[slot]
            lh = qh_[:, m * 128:(m + 1) * 128]
            ll = ql_[:, m * 128:(m + 1) * 128]
            row = scpn[par]
            first = True
            last_tok = None
            for w in range(W):
                p.wait("tensor", stb["sbank"][w % 2])
                pj = PA if (w % 2 == 0) else PB
                tok_t = p.mm3(pj[:, 0:512], lh, ll,
                              kh_[:, w * 512:(w + 1) * 512], kl_[:, w * 512:(w + 1) * 512],
                              start=True, stop=True, inc=True)
                last_tok = tok_t
                if w == W - 1:
                    p.wait("vector", tok_t)
                    p.v(lambda e, pj=pj: e.tensor_tensor(pj[:, 0:512], pj[:, 0:512],
                                                         mask_sb[:], OP.add))
                    tok_t = p.pub("vector")
                p.wait("scalar", tok_t)
                if first:
                    p.wait("scalar", stb["scpn_free"][par])
                    first = False
                p.s(lambda e, pj=pj, w=w, row=row:
                    e.activation(row[:, w * 512:(w + 1) * 512], pj[:, 0:512], AF.Copy))
                tok_s = p.pub("scalar")
                stb["sbank"][w % 2] = tok_s
                yield
            if m == 7:
                stb["qk_free"][slot] = last_tok
            p.wait("vector", tok_s)
            p.v(lambda e, W=W, row=row: e.tensor_reduce(mxc[par][:], row[:, 0:W * 512],
                                                        AX.X, OP.max))
            p.vdrain()
            p.v(lambda e: e.tensor_scalar(ngc[par][:], mxc[par][:], -1.0, None, OP.mult))
            tok_v = p.pub("vector")
            p.wait("scalar", tok_v)
            p.s(lambda e, W=W, row=row: e.activation(row[:, 0:W * 512], row[:, 0:W * 512],
                                                     AF.Exp, bias=ngc[par][:, 0:1],
                                                     accum_out=dnc[par][:, 0:1]))
            tok_s = p.pub("scalar")
            stb["exp_tok"][par] = tok_s
            p.wait("vector", tok_s)
            p.v(lambda e: e.reciprocal(rcc[par][:], dnc[par][:]))
            stb["rcp_tok"][par] = p.pub("vector")
            yield

        def back_steps(h_, m):
            par = m % 2
            W = m + 1
            row = scpn[par]
            pth, ptl = ptb_hi[par], ptb_lo[par]
            p.wait("tensor", stb["exp_tok"][par])
            tok_v = None
            tok_t = None
            for w in range(W):
                p.wait("tensor", stb["tbank"][w % 2])
                pj = PC if (w % 2 == 0) else PD
                for q4 in range(4):
                    p.transpose(pj[:, q4 * 128:(q4 + 1) * 128],
                                row[:, w * 512 + q4 * 128: w * 512 + (q4 + 1) * 128],
                                ident[:])
                tok_t = p.tpub()
                p.wait("vector", tok_t)
                if w == 0:
                    p.wait("vector", stb["pt_free"][par])
                p.v(lambda e, pj=pj, w=w, pth=pth: e.tensor_copy(
                    pth[:, w * 512:(w + 1) * 512], pj[:, 0:512]))
                p.v(lambda e, pj=pj, w=w, pth=pth, ptl=ptl: e.tensor_tensor(
                    ptl[:, w * 512:(w + 1) * 512], pj[:, 0:512],
                    pth[:, w * 512:(w + 1) * 512], OP.subtract))
                tok_v = p.pub("vector")
                stb["tbank"][w % 2] = tok_v
                yield
            stb["scpn_free"][par] = tok_t
            p.wait("tensor", tok_v)
            yb = (PE_, PF)[par]
            p.wait("tensor", stb["ybank"][par])
            for kt in range(4 * W):
                p.mm3(yb[:, 0:128],
                      vb_hi[:, kt * 128:(kt + 1) * 128], vb_lo[:, kt * 128:(kt + 1) * 128],
                      pth[:, kt * 128:(kt + 1) * 128], ptl[:, kt * 128:(kt + 1) * 128],
                      start=(kt == 0), stop=(kt == 4 * W - 1))
            tok_av = p.tpub()
            stb["pt_free"][par] = tok_av
            if m == 7:
                stb["v_free"] = tok_av
            p.wait("tensor", stb["rcp_tok"][par])
            p.wait("tensor", stb["dbank"])
            p.transpose(PH[0:1, 0:128], rcc[par][:], ident[:])
            tok_t = p.tpub()
            p.wait("vector", tok_t)
            p.wait("vector", stb["bcp"])
            p.v(lambda e: e.tensor_copy(bcpad[0:1, 0:128], PH[0:1, 0:128]))
            tok_v = p.pub("vector")
            p.wait("tensor", tok_v)
            tok_bc = bcast_row(PG[:, 0:128], 128)
            stb["bcp"] = tok_bc
            p.wait("vector", tok_bc)
            p.wait("vector", tok_av)
            dst = h_ * TOWN + m * 128
            p.v(lambda e, yb=yb: e.tensor_copy(yg_[:], yb[:, 0:128]))
            p.vdrain()
            p.v(lambda e: e.tensor_tensor(yf_[:], yg_[:], PG[:, 0:128], OP.mult))
            p.vdrain()
            p.v(lambda e, dst=dst: e.tensor_copy(y_hi[:, dst:dst + 128], yf_[:]))
            p.vdrain()
            p.v(lambda e, dst=dst: e.tensor_tensor(y_lo[:, dst:dst + 128], yf_[:],
                                                   y_hi[:, dst:dst + 128], OP.subtract))
            tok_v = p.pub("vector")
            stb["ybank"][par] = tok_v
            stb["dbank"] = tok_v
            yield

        def drive_b(*gens):
            gens = [g_ for g_ in gens if g_ is not None]
            while gens:
                nxt = []
                for g_ in gens:
                    try:
                        next(g_)
                        nxt.append(g_)
                    except StopIteration:
                        pass
                gens = nxt

        for h_ in range(H):
            slot = h_ % 2
            if h_ < H - 1:
                ns = (h_ + 1) % 2
                p.wait("sync", stb["qk_free"][ns])
                stb["qk_tok"][ns] = load_qk(h_ + 1, ns)
            p.wait("tensor", stb["qk_tok"][slot])
            p.wait("tensor", stb["v_tok"])
            for m in range(8):
                drive_b(sm_steps(h_, m, slot),
                        back_steps(h_, m - 1) if m >= 1 else None)
            drive_b(back_steps(h_, 7))
            if h_ < H - 1:
                p.wait("sync", stb["v_free"])
                stb["v_tok"] = load_v(h_ + 1)
        p.barrier()
        stB.close()

        if debug:
            stD = ExitStack()
            yf = sb("yf_dbg", [128, TOWN], F32, stD)
            for h_ in range(H):
                p.v(lambda e, h_=h_: e.tensor_tensor(yf[:], y_hi[:, h_ * TOWN:(h_ + 1) * TOWN],
                                                     y_lo[:, h_ * TOWN:(h_ + 1) * TOWN], OP.add))
                tok_v = p.pub("vector")
                p.wait("scalar", tok_v)
                tok_d = p.dma("scalar", dbg["y"][h_ * 128:(h_ + 1) * 128, :], yf[:])
                p.wait("vector", tok_d)
            p.barrier()
            stD.close()

    s_x1 = dram_tmp("s_x1", [ND, 128, TOWN], F32)
    if stage >= 5:
        stO = ExitStack()
        ow_ld = [sb(f"owld{i}", [128, ND * 128], F32, stO) for i in range(2)]
        ow_hi = sb("ow_hi", [128, ND * 128], BF16, stO)
        ow_lo = sb("ow_lo", [128, ND * 128], BF16, stO)
        xof = sb("xof", [128, 512], F32, stO)
        x1t = sb("x1t", [128, TOWN], F32, stO)
        x_tiles2 = xT.rearrange("(c p) s -> c p s", p=128)

        def load_ow(ec, slot):
            return p.dma("sync", ow_ld[slot][:], wout_t[ec], ch=("d0", "d1")[slot])

        ow_tok = [load_ow(0, 0), None]
        ow_free = [None, None]
        owsp_free = None
        obank_free = [None, None]
        xof_free = None
        x1t_free = None
        for ec in range(ND):
            slot = ec % 2
            if ec < ND - 1:
                ns = (ec + 1) % 2
                p.wait("sync", ow_free[ns])
                ow_tok[ns] = load_ow(ec + 1, ns)
            p.wait("gpsimd", ow_tok[slot])
            p.wait("gpsimd", owsp_free)
            p.g(lambda e, slot=slot: e.tensor_copy(ow_hi[:], ow_ld[slot][:]))
            p.g(lambda e, slot=slot: e.tensor_tensor(ow_lo[:], ow_ld[slot][:], ow_hi[:],
                                                     OP.subtract))
            tok_v = p.pub("gpsimd")
            ow_free[slot] = tok_v
            p.wait("tensor", tok_v)
            for qc in range(2):
                p.wait("tensor", obank_free[qc])
                pj = PA if qc == 0 else PB
                for k in range(ND):
                    p.mm3(pj[:, 0:512],
                          ow_hi[:, k * 128:(k + 1) * 128], ow_lo[:, k * 128:(k + 1) * 128],
                          y_hi[:, k * TOWN + qc * 512: k * TOWN + (qc + 1) * 512],
                          y_lo[:, k * TOWN + qc * 512: k * TOWN + (qc + 1) * 512],
                          start=(k == 0), stop=(k == ND - 1))
                tok_t = p.tpub()
                tok_x = []
                for t8 in range(4):
                    p.wait("sync", xof_free)
                    tok_x.append(p.dma("sync", xof[:, t8 * 128:(t8 + 1) * 128],
                                       x_tiles2[ec][:, (qc * 4 + t8) * 512:
                                                    (qc * 4 + t8) * 512 + 128], ch="d2"))
                p.wait("vector", tok_t)
                for t in tok_x:
                    p.wait("vector", t)
                if qc == 0:
                    p.wait("vector", x1t_free)
                p.v(lambda e, pj=pj, ec=ec, qc=qc:
                    e.tensor_scalar(x1t[:, qc * 512:(qc + 1) * 512], pj[:, 0:512],
                                    ga_cl[:, ec:ec + 1], None, OP.mult))
                p.v(lambda e, qc=qc: e.tensor_tensor(x1t[:, qc * 512:(qc + 1) * 512],
                                                     x1t[:, qc * 512:(qc + 1) * 512],
                                                     xof[:], OP.add))
                tok_v = p.pub("vector")
                obank_free[qc] = tok_v
                xof_free = tok_v
            owsp_free = tok_t
            p.wait("scalar", tok_v)
            x1t_free = p.dma("scalar", s_x1[ec], x1t[:])
        p.barrier()
        stO.close()
    stBO.close()

    x1 = sb("x1", [128, ND * TOWN], F32)
    p.v(lambda e: e.memset(x1[:], 0.0))
    tok_x1init = p.pub("vector")
    if stage >= 5:
        p.wait("sync", tok_x1init)
        tk = None
        for k in range(ND):
            tk = p.dma("sync", x1[:, k * TOWN:(k + 1) * TOWN], s_x1[k])
        p.wait("vector", tk)
        p.wait("tensor", tk)
        p.wait("scalar", tk)
        if debug:
            for k in range(ND):
                tok_d = p.dma("scalar", dbg["x1"][k * 128:(k + 1) * 128, :],
                              x1[:, k * TOWN:(k + 1) * TOWN])
            p.wait("scalar", tok_d)
            p.barrier()

    if stage >= 5:
        stF = ExitStack()
        h2_hi = sb("h2_hi", [128, ND * 512], BF16, stF)
        h2_lo = sb("h2_lo", [128, ND * 512], BF16, stF)
        l_hi = sb("l_hi", [128, 512], BF16, stF)
        l_lo = sb("l_lo", [128, 512], BF16, stF)
        l2f = sb("l2f", [128, 512], F32, stF)
        l2_hi = sb("l2_hi", [128, 512], BF16, stF)
        l2_lo = sb("l2_lo", [128, 512], BF16, stF)
        lh1f = sb("lh1f", [128, 512], F32, stF)
        lmean = sb("lmean", [1, 512], F32, stF)
        lvar = sb("lvar", [1, 512], F32, stF)
        lrstd = sb("lrstd", [1, 512], F32, stF)
        lstd = sb("lstd", [1, 512], F32, stF)
        sc_tm = sb("sc_tm", [128, 4 * NE], F32, stF)
        pick1 = sb("g_pick1", [128, 4 * NE], F32, stF)
        pick2 = sb("g_pick2", [128, 4 * NE], F32, stF)
        scm = sb("g_scm", [128, 4 * NE], F32, stF)
        fnd = sb("g_fnd", [128, 4], F32, stF)
        mrow = sb("g_mrow", [128, 4], F32, stF)
        m_rows = sb("m_rows", [1, NE * 512], F32, stF)
        w13_ld = [sb(f"w13ld{i}", [128, ND * 128], F32, stF) for i in range(2)]
        w13_hi = [sb(f"w13hi{i}", [128, ND * 128], BF16, stF) for i in range(2)]
        w13_lo = [sb(f"w13lo{i}", [128, ND * 128], BF16, stF) for i in range(2)]
        w2_ld = [sb(f"w2ld{i}", [128, 8 * 128], F32, stF) for i in range(2)]
        w2s_hi = [sb(f"w2s_hi{i}", [128, 8 * 128], BF16, stF) for i in range(2)]
        w2s_lo = [sb(f"w2s_lo{i}", [128, 8 * 128], BF16, stF) for i in range(2)]
        g_hi = sb("g_hi", [128, 8 * 512], BF16, stF)
        g_lo = sb("g_lo", [128, 8 * 512], BF16, stF)
        mb = sb("mb", [128, 512], F32, stF)
        af = sb("af", [128, 512], F32, stF)
        bf = l2f
        sf = lh1f

        def load_w13(e_, ft, which, slot):
            idx = (2 * e_ + which) if e_ < NE else (16 + which)
            return p.dma("sync", w13_ld[slot][:], w13_t[idx, ft], ch=("d0", "d1")[slot])

        def load_w2(e_, dc, slot):
            idx = e_ if e_ < NE else NE
            return p.dma("sync", w2_ld[slot][:], w2_t[idx, dc], ch=("d2", "d3")[slot])

        def ecol(buf, e_):
            return buf[:].rearrange("p (t e) -> p t e", e=NE)[:, :, e_:e_ + 1]

        gta = sb("gta", [128, 4], F32, stF)
        gtb = sb("gtb", [128, 4], F32, stF)
        gtc = sb("gtc", [128, 4], F32, stF)
        gfa = sb("gfa", [128, 4], F32, stF)
        gfb = sb("gfb", [128, 4], F32, stF)
        gtmp = sb("gtmp", [128, 4 * NE], F32, stF)

        w13_free = [None, None]
        w2_free = [None, None]
        hbank_free = [None, None]
        rbank_free = [None, None]
        mbank2_free = None
        mb_free = None
        h2_free = None
        for tc in range(2):
            xoff = tc * 512
            tok_stats = None
            tok_v2 = None
            p.wait("vector", h2_free)
            for k in range(ND):
                xc = x1[:, k * TOWN + xoff: k * TOWN + xoff + 512]
                p.v(lambda e, xc=xc: e.tensor_copy(l_hi[:], xc))
                p.v(lambda e, xc=xc: e.tensor_tensor(l_lo[:], xc, l_hi[:], OP.subtract))
                p.wait("scalar", tok_v2)
                p.s(lambda e, xc=xc: e.activation(l2f[:], xc, AF.Square))
                tok_s = p.pub("scalar")
                p.wait("vector", tok_s)
                p.v(lambda e: e.tensor_copy(l2_hi[:], l2f[:]))
                p.v(lambda e: e.tensor_tensor(l2_lo[:], l2f[:], l2_hi[:], OP.subtract))
                tok_v2 = p.pub("vector")
                p.wait("tensor", tok_v2)
                p.mm(PG[0:1, 0:512], ones_m1[:], l_hi[:], k == 0, False)
                p.mm(PG[0:1, 0:512], ones_m1[:], l_lo[:], False, k == ND - 1)
                p.mm(PH[0:1, 0:512], ones_m1[:], l2_hi[:], k == 0, False)
                tok_stats = p.mm(PH[0:1, 0:512], ones_m1[:], l2_lo[:], False, k == ND - 1,
                                 inc=(k == ND - 1))
                tok_t2 = p.tpub()
                p.wait("vector", tok_t2)
            p.wait("vector", tok_stats)
            p.v(lambda e: e.tensor_scalar(lmean[:], PG[0:1, 0:512], 1.0 / D, None, OP.mult))
            p.v(lambda e: e.tensor_scalar(lvar[:], PH[0:1, 0:512], 1.0 / D, None, OP.mult))
            p.vdrain()
            p.v(lambda e: e.tensor_tensor(l2f[0:1, 0:512], lmean[:], lmean[:], OP.mult))
            p.vdrain()
            p.v(lambda e: e.tensor_tensor(lvar[:], lvar[:], l2f[0:1, 0:512], OP.subtract))
            tok_v = p.pub("vector")
            p.wait("scalar", tok_v)
            p.s(lambda e: e.activation(lstd[:], lvar[:], AF.Sqrt,
                                                   bias=eps_col[0:1, 0:1]))
            tok_s = p.pub("scalar")
            p.wait("vector", tok_s)
            p.v(lambda e: e.reciprocal(lrstd[:], lstd[:]))
            p.v(lambda e: e.tensor_copy(bcpad[0:1, 0:512], lmean[:]))
            tok_v = p.pub("vector")
            p.wait("tensor", tok_v)
            bcast_row(PG[:, 0:512], 512)
            tok_t = p.tpub()
            p.wait("vector", tok_t)
            p.v(lambda e: e.tensor_copy(bcpad[0:1, 0:512], lrstd[:]))
            tok_v = p.pub("vector")
            p.wait("tensor", tok_v)
            tok_t = bcast_row(PH[:, 0:512], 512)
            p.wait("vector", tok_t)
            for k in range(ND):
                xc = x1[:, k * TOWN + xoff: k * TOWN + xoff + 512]
                dst = k * 512
                p.v(lambda e, xc=xc: e.tensor_tensor(lh1f[:], xc, PG[:, 0:512], OP.subtract))
                p.v(lambda e: e.tensor_tensor(lh1f[:], lh1f[:], PH[:, 0:512], OP.mult))
                p.v(lambda e, k=k: e.tensor_scalar(lh1f[:], lh1f[:], cb1[:, k:k + 1],
                                                   mod_sb[:, 48 + k:48 + k + 1], OP.mult, OP.add))
                p.v(lambda e, dst=dst: e.tensor_copy(h2_hi[:, dst:dst + 512], lh1f[:]))
                p.v(lambda e, dst=dst: e.tensor_tensor(h2_lo[:, dst:dst + 512], lh1f[:],
                                                       h2_hi[:, dst:dst + 512], OP.subtract))
            tok_h2 = p.pub("vector")
            p.wait("tensor", tok_h2)
            if debug:
                p.wait("scalar", tok_h2)
                for k in range(ND):
                    p.v(lambda e, k=k: e.tensor_tensor(lh1f[:], h2_hi[:, k * 512:(k + 1) * 512],
                                                       h2_lo[:, k * 512:(k + 1) * 512], OP.add))
                    tok_dv = p.pub("vector")
                    p.wait("scalar", tok_dv)
                    tok_d = p.dma("scalar", dbg["h2"][k * 128:(k + 1) * 128, xoff:xoff + 512],
                                  lh1f[:])
                    p.wait("vector", tok_d)

            p.wait("tensor", tok_gw)
            for t4 in range(4):
                for k in range(ND):
                    p.mm3(PA[0:128, 0:NE],
                          h2_hi[:, k * 512 + t4 * 128: k * 512 + (t4 + 1) * 128],
                          h2_lo[:, k * 512 + t4 * 128: k * 512 + (t4 + 1) * 128],
                          gw_hi[:, k * NE:(k + 1) * NE], gw_lo[:, k * NE:(k + 1) * NE],
                          start=(k == 0), stop=(k == ND - 1))
                tok_t = p.tpub()
                p.wait("scalar", tok_t)
                p.s(lambda e, t4=t4: e.activation(sc_tm[:, t4 * NE:(t4 + 1) * NE],
                                                              PA[0:128, 0:NE], AF.Sigmoid))
                tok_s = p.pub("scalar")
                p.wait("tensor", tok_s)
            tok_s = p.pub("scalar")
            p.wait("vector", tok_s)
            p.v(lambda e: e.tensor_copy(scm[:], sc_tm[:]))
            for it, pk in enumerate((pick1, pick2)):
                p.v(lambda e: e.tensor_reduce(mrow[:],
                                              scm[:].rearrange("p (t e) -> p t e", e=NE),
                                              AX.X, OP.max))
                p.vdrain()
                fpair = (gfa, gfb)
                p.v(lambda e: e.memset(gfa[:], 0.0))
                for e_ in range(NE):
                    fc, fn_ = fpair[e_ % 2], fpair[(e_ + 1) % 2]
                    p.v(lambda e, e_=e_: e.tensor_tensor(
                        gta[:].unsqueeze(2), ecol(scm, e_), mrow[:].unsqueeze(2), OP.is_ge))
                    p.vdrain()
                    p.v(lambda e, fc=fc: e.scalar_tensor_tensor(
                        gtb[:], fc[:], -1.0, gta[:], OP.mult, OP.add))
                    p.vdrain()
                    p.v(lambda e: e.tensor_relu(gtc[:], gtb[:]))
                    p.vdrain()
                    p.v(lambda e, e_=e_, pk=pk: e.tensor_copy(ecol(pk, e_),
                                                              gtc[:].unsqueeze(2)))
                    p.v(lambda e, fc=fc, fn_=fn_: e.tensor_tensor(fn_[:], fc[:], gtc[:],
                                                                  OP.add))
                    p.vdrain()
                if it == 0:
                    p.vdrain()
                    p.v(lambda e: e.scalar_tensor_tensor(scm[:], pick1[:], NEG, sc_tm[:],
                                                         OP.mult, OP.add))
                    p.vdrain()
            p.vdrain()
            p.v(lambda e: e.tensor_tensor(gtmp[:], pick1[:], pick2[:], OP.add))
            p.vdrain()
            p.v(lambda e: e.tensor_tensor(scm[:], sc_tm[:], gtmp[:], OP.mult))
            tok_v = p.pub("vector")
            p.wait("tensor", tok_v)
            tok_m = None
            for ee in range(NE):
                pj = PC if (ee % 2 == 0) else PD
                for t4 in range(4):
                    p.transpose(pj[0:1, t4 * 128:(t4 + 1) * 128],
                                scm[:, t4 * NE + ee: t4 * NE + ee + 1], ident[:])
                tok_t = p.tpub()
                p.wait("vector", tok_t)
                p.v(lambda e, ee=ee, pj=pj: e.tensor_copy(
                    m_rows[0:1, ee * 512:(ee + 1) * 512], pj[0:1, 0:512]))
                tok_m = p.pub("vector")
                p.wait("tensor", tok_m)
            if debug:
                p.wait("scalar", tok_m)
                for ee in range(NE):
                    tok_d = p.dma("scalar", dbg["m"][ee:ee + 1, xoff:xoff + 512],
                                  m_rows[0:1, ee * 512:(ee + 1) * 512])
                p.wait("scalar", tok_d)

            for e_ in range(NE + 1):
                routed = e_ < NE
                if routed:
                    p.wait("vector", mb_free)
                    p.v(lambda e, e_=e_: e.tensor_copy(bcpad[0:1, 0:512],
                                                       m_rows[0:1, e_ * 512:(e_ + 1) * 512]))
                    tok_v = p.pub("vector")
                    p.wait("tensor", tok_v)
                    p.wait("tensor", mbank2_free)
                    tok_t = bcast_row(PG[:, 0:512], 512)
                    p.wait("vector", tok_t)
                    p.v(lambda e: e.tensor_copy(mb[:], PG[:, 0:512]))
                    tok_v = p.pub("vector")
                    mbank2_free = tok_v
                    p.wait("tensor", tok_v)
                for ft in range(8):
                    for which in range(2):
                        slot = which
                        p.wait("sync", w13_free[slot])
                        tok_w = load_w13(e_, ft, which, slot)
                        p.wait("gpsimd", tok_w)
                        p.wait("gpsimd", w13_free[slot])
                        p.g(lambda e, slot=slot: e.tensor_copy(w13_hi[slot][:], w13_ld[slot][:]))
                        p.g(lambda e, slot=slot: e.tensor_tensor(w13_lo[slot][:],
                                                                 w13_ld[slot][:],
                                                                 w13_hi[slot][:], OP.subtract))
                        tok_v = p.pub("gpsimd")
                        p.wait("tensor", tok_v)
                        if which == 0:
                            p.wait("tensor", hbank_free[ft % 2])
                        pj = (PA, PB)[ft % 2] if which == 0 else (PC, PD)[ft % 2]
                        for k in range(ND):
                            p.mm3(pj[:, 0:512],
                                  w13_hi[slot][:, k * 128:(k + 1) * 128],
                                  w13_lo[slot][:, k * 128:(k + 1) * 128],
                                  h2_hi[:, k * 512:(k + 1) * 512],
                                  h2_lo[:, k * 512:(k + 1) * 512],
                                  start=(k == 0), stop=(k == ND - 1))
                        tok_t = p.tpub()
                        w13_free[slot] = tok_t
                    p.wait("vector", tok_t)
                    p1 = (PA, PB)[ft % 2]
                    p3 = (PC, PD)[ft % 2]
                    gdst = ft * 512
                    if routed:
                        p.v(lambda e, p1=p1: e.tensor_tensor(af[:], p1[:, 0:512], mb[:],
                                                             OP.mult))
                        tok_v = p.pub("vector")
                        p.wait("scalar", tok_v)
                        p.s(lambda e: e.activation(sf[:], af[:], AF.Silu))
                        tok_s = p.pub("scalar")
                        p.wait("vector", tok_s)
                        p.v(lambda e, p3=p3: e.tensor_tensor(bf[:], p3[:, 0:512], mb[:],
                                                             OP.mult))
                        p.v(lambda e: e.tensor_tensor(af[:], sf[:], bf[:], OP.mult))
                    else:
                        tok_v = p.pub("vector")
                        p.wait("scalar", tok_v)
                        p.s(lambda e, p1=p1: e.activation(sf[:], p1[:, 0:512],
                                                                      AF.Silu))
                        tok_s = p.pub("scalar")
                        p.wait("vector", tok_s)
                        p.v(lambda e, p3=p3: e.tensor_tensor(af[:], sf[:], p3[:, 0:512],
                                                             OP.mult))
                    p.v(lambda e, gdst=gdst: e.tensor_copy(g_hi[:, gdst:gdst + 512], af[:]))
                    p.v(lambda e, gdst=gdst: e.tensor_tensor(g_lo[:, gdst:gdst + 512], af[:],
                                                             g_hi[:, gdst:gdst + 512],
                                                             OP.subtract))
                    tok_v = p.pub("vector")
                    hbank_free[ft % 2] = tok_v
                    p.wait("scalar", tok_v)
                w2_tok = [None, None]
                w2spl_free = [None, None]
                p.wait("tensor", hbank_free[0])
                p.wait("tensor", hbank_free[1])
                p.wait("sync", w2_free[0])
                w2_tok[0] = load_w2(e_, 0, 0)
                for dc in range(ND):
                    slot = dc % 2
                    if dc < ND - 1:
                        ns = (dc + 1) % 2
                        p.wait("sync", w2_free[ns])
                        w2_tok[ns] = load_w2(e_, dc + 1, ns)
                    p.wait("gpsimd", w2_tok[slot])
                    p.wait("gpsimd", w2spl_free[slot])
                    p.g(lambda e, slot=slot: e.tensor_copy(w2s_hi[slot][:], w2_ld[slot][:]))
                    p.g(lambda e, slot=slot: e.tensor_tensor(w2s_lo[slot][:], w2_ld[slot][:],
                                                             w2s_hi[slot][:], OP.subtract))
                    tok_v = p.pub("gpsimd")
                    w2_free[slot] = tok_v
                    p.wait("tensor", tok_v)
                    p.wait("tensor", rbank_free[dc % 2])
                    pj = (PE_, PF)[dc % 2]
                    for fc in range(8):
                        p.mm3(pj[:, 0:512],
                              w2s_hi[slot][:, fc * 128:(fc + 1) * 128],
                              w2s_lo[slot][:, fc * 128:(fc + 1) * 128],
                              g_hi[:, fc * 512:(fc + 1) * 512],
                              g_lo[:, fc * 512:(fc + 1) * 512],
                              start=(fc == 0), stop=(fc == 7))
                    tok_t = p.tpub()
                    w2spl_free[slot] = tok_t
                    p.wait("vector", tok_t)
                    dst = dc * TOWN + xoff
                    p.v(lambda e, dc=dc, pj=pj: e.tensor_scalar(af[:], pj[:, 0:512],
                                                                gb_cl[:, dc:dc + 1], None,
                                                                OP.mult))
                    p.v(lambda e, dst=dst: e.tensor_tensor(x1[:, dst:dst + 512],
                                                           x1[:, dst:dst + 512], af[:],
                                                           OP.add))
                    tok_v = p.pub("vector")
                    rbank_free[dc % 2] = tok_v
                    mb_free = tok_v
                hbank_free = [rbank_free[0], rbank_free[1]]
            h2_free = rbank_free[1]
        p.barrier()
        stF.close()

    p.barrier()
    last = None
    for k in range(ND):
        last = p.dma("scalar", out_d[k * 128:(k + 1) * 128, :], x1[:, k * TOWN:(k + 1) * TOWN])
    p.wait("scalar", last)
    p.wait("sync", last)

    with nc.Block() as block:
        p.emit(block)
    st.close()
    return nc


_NC_CACHE = {}


def _get_nc(debug=False, stage=None):
    key = (debug, stage if stage is not None else STAGE)
    if key not in _NC_CACHE:
        _NC_CACHE[key] = build_program(debug=debug, stage=stage)
    return _NC_CACHE[key]


def prep_inputs(inputs, c):
    b, j = c // 4, c % 4
    x = np.ascontiguousarray(np.asarray(inputs["L0_attn_x"])[b])
    others = [i for i in range(4) if i != j]
    perm = []
    for t in range(8):
        perm.append(4 * t + j)
        perm.extend(4 * t + o for o in others)
    perm = np.array(perm)
    cols = (perm[:, None] * 128 + np.arange(128)[None, :]).ravel()
    xT = np.ascontiguousarray(x.T[:, cols])

    tri = np.where(np.arange(128)[None, :] <= np.arange(128)[:, None], 0.0, NEG).astype(np.float32)
    mq = np.zeros((128, 4, 128), np.float32)
    mq[:, 0, :] = tri
    for i in range(1, 4):
        mq[:, i, :] = 0.0 if others[i - 1] < j else NEG
    return {
        "xT": xT.astype(np.float32),
        "c_fm_d": np.ascontiguousarray(np.asarray(inputs["L0_c"])[b].reshape(16, 128).T).astype(np.float32),
        "maskq": np.ascontiguousarray(mq.reshape(128, 512)),
    }


def prep_shared(inputs):
    w_qkv = np.asarray(inputs["L0_attn_w_qkv"])
    w_out = np.asarray(inputs["L0_attn_w_out"])
    w1 = np.asarray(inputs["L0_ffn_w1"])
    w2 = np.asarray(inputs["L0_ffn_w2"])
    w3 = np.asarray(inputs["L0_ffn_w3"])
    ws1 = np.asarray(inputs["L0_ffn_ws1"])
    ws2 = np.asarray(inputs["L0_ffn_ws2"])
    ws3 = np.asarray(inputs["L0_ffn_ws3"])

    woutT = w_out.T
    wout_t = np.ascontiguousarray(
        woutT.reshape(ND, 128, ND, 128).transpose(2, 1, 0, 3).reshape(ND, 128, ND * 128))
    blocks = []
    for e in range(NE):
        for w in (w1[e], w3[e]):
            blocks.append(w.T.reshape(ND, 128, FH // 128, 128).transpose(2, 1, 0, 3)
                          .reshape(FH // 128, 128, ND * 128))
    for w in (ws1, ws3):
        blocks.append(w.T.reshape(ND, 128, FH // 128, 128).transpose(2, 1, 0, 3)
                      .reshape(FH // 128, 128, ND * 128))
    w13_t = np.ascontiguousarray(np.stack(blocks, 0))
    blocks = []
    for e in range(NE):
        blocks.append(w2[e].T.reshape(FH // 128, 128, ND, 128).transpose(2, 1, 0, 3)
                      .reshape(ND, 128, FH // 128 * 128))
    blocks.append(ws2.T.reshape(FH // 128, 128, ND, 128).transpose(2, 1, 0, 3)
                  .reshape(ND, 128, FH // 128 * 128))
    w2_t = np.ascontiguousarray(np.stack(blocks, 0))

    return {
        "wmod_d": np.ascontiguousarray(np.asarray(inputs["L0_wmod"])).astype(np.float32),
        "wqkv_t": np.ascontiguousarray(
            w_qkv.T.reshape(ND, 128, 48, 128).transpose(2, 1, 0, 3)
            .reshape(48, 128, ND * 128)).astype(np.float32),
        "wout_t": wout_t.astype(np.float32),
        "w13_t": w13_t.astype(np.float32),
        "w2_t": w2_t.astype(np.float32),
        "gwT": np.ascontiguousarray(np.asarray(inputs["L0_ffn_gate_w"]).T).astype(np.float32),
        "eye": np.eye(128, dtype=np.float32),
    }


def run(inputs, debug=False, stage=None):
    nc = _get_nc(debug=debug, stage=stage)
    shared = prep_shared(inputs)
    in_maps = []
    for c in range(8):
        m = dict(shared)
        m.update(prep_inputs(inputs, c))
        in_maps.append(m)
    return run_bass_kernel_spmd(nc, in_maps, core_ids=list(range(8)))


def assemble(results, inputs):
    out = np.zeros((2, S, D), np.float32)
    for c in range(8):
        b, j = c // 4, c % 4
        o = results[c]["out"]
        rows = (np.array([j + 4 * mm for mm in range(8)])[:, None] * 128
                + np.arange(128)[None, :]).ravel()
        out[b, rows, :] = o.T
    return out


def kernel(**inputs):
    res = run(inputs, debug=False)
    return assemble(res.results, inputs)
